# version 16
# baseline (speedup 1.0000x reference)
import numpy as np
import concourse.bass as bass
import concourse.mybir as mybir
from concourse.bass_utils import run_bass_kernel_spmd

F32 = mybir.dt.float32
BF16 = mybir.dt.bfloat16
I32 = mybir.dt.int32
AF = mybir.ActivationFunctionType
AX = mybir.AxisListType
OP = mybir.AluOpType

NCORES = 8
T = 2048
D = 1024
HID = 3072
NSEQ = 2
TOK = NSEQ * T
CPS = T // 128
EPS = 1e-5
QS = 1.0 / 127.0
KB = 1024


class Marker:
    __slots__ = ("sem", "val")

    def __init__(self, sem, val):
        self.sem = sem
        self.val = val


class Prog:

    ENGINES = ("sync", "gpsimd", "vector", "scalar", "tensor")

    KPOOL = {"s": 14, "g": 14, "a": 48}

    def __init__(self, nc):
        self.nc = nc
        self.ops = {e: [] for e in self.ENGINES}
        self.sems = {}
        for name in ("te", "dv", "ac", "sy", "gp"):
            h = nc.semaphore(name).__enter__()
            self.sems[name] = [h, 0]
        for q in ("s", "g", "a"):
            for j in range(self.KPOOL[q]):
                name = f"q{q}{j}"
                h = nc.semaphore(name).__enter__()
                self.sems[name] = [h, 0]
        self.qprefix = {"sync": "qs", "gpsimd": "qg", "scalar": "qa"}
        self.qcount = {"sync": 0, "gpsimd": 0, "scalar": 0}
        self.csem = {"vector": "dv", "scalar": "ac", "tensor": "te",
                     "sync": "sy", "gpsimd": "gp"}
        self.pending_fence = {e: [] for e in self.ENGINES}

    def cur(self, name):
        return Marker(name, self.sems[name][1])

    def queue_tail(self, queue):
        pre = self.qprefix[queue]
        return [self.cur(f"{pre}{j}") for j in range(self.KPOOL[pre[1]])
                if self.sems[f"{pre}{j}"][1] > 0]

    def fence(self, ms=None):
        if ms is None:
            ms = self.fence_markers()
        for e in self.ENGINES:
            self.pending_fence[e] = ms

    def fence_markers(self):
        return ([self.cur(n) for n in ("te", "dv", "ac")]
                + self.queue_tail("sync") + self.queue_tail("gpsimd")
                + self.queue_tail("scalar"))

    def _take_fence(self, eng):
        f = self.pending_fence[eng]
        self.pending_fence[eng] = []
        return f

    def dma(self, queue, out, in_, waits=(), **kw):
        i = self.qcount[queue]
        self.qcount[queue] = i + 1
        name = f"{self.qprefix[queue]}{i % self.KPOOL[self.qprefix[queue][1]]}"
        prev = self.sems[name][1]
        self.sems[name][1] = prev + 16
        val = prev + 16
        h = self.sems[name][0]
        waits = list(waits)
        if prev > 0:
            waits.append(Marker(name, prev))

        def fn(e, out=out, in_=in_, kw=kw, h=h):
            return e.dma_start(out=out, in_=in_, **kw).then_inc(h, 16)

        eng = "scalar" if queue == "scalar" else queue
        self.ops[eng].append((fn, self._resolve(waits, eng), None))
        return Marker(name, val)

    AUTOCHAIN = ("vector", "scalar")

    def op(self, eng, fn, waits=(), inc=True):
        m = None
        incpair = None
        name = self.csem[eng]
        waits = list(waits)
        if eng in self.AUTOCHAIN and self.sems[name][1] > 0:
            waits.append(Marker(name, self.sems[name][1]))
        if inc:
            self.sems[name][1] += 1
            m = Marker(name, self.sems[name][1])
            incpair = (self.sems[name][0], 1)
        self.ops[eng].append((fn, self._resolve(waits, eng), incpair))
        return m

    def _resolve(self, waits, eng):
        out = []
        for w in list(waits) + self._take_fence(eng):
            if w is None:
                continue
            if isinstance(w, (list, tuple)):
                out.extend((m.sem, m.val) for m in w if m is not None)
            else:
                out.append((w.sem, w.val))
        return out

    def emit(self, block):
        def run(e, eng_name):
            own = self.csem.get(eng_name)
            skip_own = eng_name not in self.AUTOCHAIN
            ew = {}
            for fn, waits, incpair in self.ops[eng_name]:
                for sem_name, val in waits:
                    if sem_name == own and skip_own:
                        continue
                    if ew.get(sem_name, 0) >= val:
                        continue
                    e.wait_ge(self.sems[sem_name][0], val)
                    ew[sem_name] = val
                inst = fn(e)
                if incpair is not None:
                    inst.then_inc(*incpair)

        block.sync(lambda e: run(e, "sync"))
        block.gpsimd(lambda e: run(e, "gpsimd"))
        block.vector(lambda e: run(e, "vector"))
        block.scalar(lambda e: run(e, "scalar"))
        block.tensor(lambda e: run(e, "tensor"))


class Res:
    def __init__(self):
        self.ready = []
        self.free = []


def build(debug=False, upto='all', nchunks=16, sub='full'):
    nc = bass.Bass("TRN2", target_bir_lowering=False)

    x_in = nc.dram_tensor("x", [TOK, D], F32, kind="ExternalInput").ap()
    ln_g = nc.dram_tensor("ln_g", [D], F32, kind="ExternalInput").ap()
    ln_b = nc.dram_tensor("ln_b", [D], F32, kind="ExternalInput").ap()
    Wq = nc.dram_tensor("Wq", [D, D], F32, kind="ExternalInput").ap()
    Wk = nc.dram_tensor("Wk", [D, D], F32, kind="ExternalInput").ap()
    Wv = nc.dram_tensor("Wv", [D, D], F32, kind="ExternalInput").ap()
    bq = nc.dram_tensor("bq", [D], F32, kind="ExternalInput").ap()
    bk = nc.dram_tensor("bk", [D], F32, kind="ExternalInput").ap()
    bv = nc.dram_tensor("bv", [D], F32, kind="ExternalInput").ap()
    wg_i = nc.dram_tensor("wg", [HID, D], I32, kind="ExternalInput").ap()
    wu_i = nc.dram_tensor("wu", [HID, D], I32, kind="ExternalInput").ap()
    wd_i = nc.dram_tensor("wd", [D, HID], I32, kind="ExternalInput").ap()
    ident_in = nc.dram_tensor("ident", [128, 128], F32, kind="ExternalInput").ap()
    out_hbm = nc.dram_tensor("out", [TOK, D], F32, kind="ExternalOutput").ap()

    ik = "ExternalOutput" if debug else "Internal"
    s_wg = nc.dram_tensor("s_wg", [HID, D], BF16, kind="Internal").ap()
    s_wu = nc.dram_tensor("s_wu", [HID, D], BF16, kind="Internal").ap()
    s_wd = nc.dram_tensor("s_wd", [D, HID], BF16, kind="Internal").ap()
    s_kT = nc.dram_tensor("s_kT", [NSEQ, 8, 128, T], F32, kind=ik).ap()
    s_v = nc.dram_tensor("s_v", [NSEQ, T, D], F32, kind=ik).ap()
    s_qT = nc.dram_tensor("s_qT", [NSEQ, 8, 128, T], F32, kind=ik).ap()
    s_r = nc.dram_tensor("s_r", [NSEQ, T, D], F32, kind=ik).ap()
    s_h1 = nc.dram_tensor("s_h1", [TOK, D], F32, kind=ik).ap()
    s_hnT_hi = nc.dram_tensor("s_hnT_hi", [8, 128, TOK], BF16, kind="Internal").ap()
    s_hnT_lo = nc.dram_tensor("s_hnT_lo", [8, 128, TOK], BF16, kind="Internal").ap()
    s_brow_k = nc.dram_tensor("s_brow_k", [D], F32, kind="Internal").ap()

    arena = nc.alloc_sbuf_tensor("arena", [128, 192 * KB], mybir.dt.uint8)
    _base = nc.lookup_mloc(arena).addr

    def A(name, shape, dtype, offset):
        return nc.alloc_sbuf_tensor_at(name, shape, dtype, offset=_base + offset)
    g_col = A("g_col", [128, 8], F32, offset=0)
    bcol_hi = A("bcol_hi", [128, 8], BF16, offset=64)
    bcol_lo = A("bcol_lo", [128, 8], BF16, offset=96)
    bk_col = A("bk_col", [128, 8], F32, offset=128)
    Mk = A("Mk", [128, 8], F32, offset=192)
    rk = A("rk", [128, 8], F32, offset=224)
    nMk = A("nMk", [128, 8], F32, offset=256)
    bcol_f = A("bcol_f", [128, 8], F32, offset=288)
    rq_all = A("rq_all", [128, CPS, 16], F32, offset=320)
    stat = A("stat", [128, 48], F32, offset=1344)
    ones_t = A("ones_t", [128, 128], F32, offset=1536)
    ident = A("ident", [128, 128], F32, offset=2048)
    ones_bf = A("ones_bf", [1, 128], BF16, offset=2560)
    bq_col = A("bq_col", [128, 8], F32, offset=2816)
    ones2 = A("ones2", [128, 2], F32, offset=2880)
    brow_v_hi = A("brow_v_hi", [1, D], BF16, offset=2816 + 4 * KB)
    brow_v_lo = A("brow_v_lo", [1, D], BF16, offset=2816 + 6 * KB)
    attn_all = A("attn_all", [128, 8, 128], F32, offset=2816 + 8 * KB)

    WOFF = 16 * KB
    wT = {}
    for i, (w, ver) in enumerate([(w, v) for w in "qkv" for v in ("hi", "lo")]):
        wT[(w, ver)] = A(f"w{w}T_{ver}", [128, 8, D], BF16, offset=WOFF + i * 16 * KB)

    P1 = 112 * KB
    x0_r = A("x0_r", [128, 2, D], F32, offset=P1)
    n1_t = A("n1_t", [128, D], F32, offset=P1 + 8 * KB)
    xn_r = A("xn_r", [128, 2, D], F32, offset=P1 + 12 * KB)
    r_r = A("r_r", [128, 2, D], F32, offset=P1 + 20 * KB)
    qT_stg = A("qT_stg", [128, 2, 512], F32, offset=P1 + 44 * KB)
    kT_stg = A("kT_stg", [128, 2, 512], F32, offset=P1 + 52 * KB)
    v_stg = A("v_stg", [128, 2, D], F32, offset=P1 + 56 * KB)
    xnT_hi = A("xnT_hi", [128, 8, 4, 128], BF16, offset=P1 + 64 * KB)
    xnT_lo = A("xnT_lo", [128, 8, 4, 128], BF16, offset=P1 + 72 * KB)

    kTe2 = A("kTe2", [128, 2, T], F32, offset=P1)
    ekTq = A("ekTq", [128, 2, 512], F32, offset=P1 + 16 * KB)
    ek_r = A("ek_r", [128, 2, 4, 128], F32, offset=P1 + 20 * KB)
    v_p2 = A("v_p2", [128, 2, 16, 128], F32, offset=P1 + 24 * KB)

    qT_c = A("qT_c", [128, 2, 8, 128], F32, offset=P1 + 40 * KB)
    r_c = A("r_c", [128, 2, D], F32, offset=P1 + 48 * KB)
    h1_st = A("h1_st", [128, 2, D], F32, offset=P1 + 56 * KB)
    h1n_t = A("h1n_t", [128, 2, D], F32, offset=P1 + 64 * KB)
    hnT_hi_st = A("hnT_hi_st", [128, 8, 256], BF16, offset=P1 + 72 * KB)
    hnT_lo_st = A("hnT_lo_st", [128, 8, 256], BF16, offset=P1 + 76 * KB)

    pw_f = A("pw_f", [128, 4, D], F32, offset=P1 + 28 * KB)
    ptmp = A("ptmp", [128, 2, D], F32, offset=P1 + 44 * KB)
    brow = {w: A(f"brow_{w}", [1, D], F32, offset=P1 + 52 * KB + i * 4 * KB)
            for i, w in enumerate("qkv")}

    wgT = A("wgT", [128, 8, HID], BF16, offset=16 * KB)
    wuT = A("wuT", [128, 8, HID], BF16, offset=64 * KB)
    hnT_hi_b = A("hnT_hi_b", [128, 2, 8, 256], BF16, offset=112 * KB)
    hnT_lo_b = A("hnT_lo_b", [128, 2, 8, 256], BF16, offset=120 * KB)
    wdT_t = A("wdT_t", [128, 24, D], BF16, offset=128 * KB)
    g1_t = A("g1_t", [128, 2, 256], F32, offset=176 * KB)
    hmT_hi = A("hmT_hi", [128, 4, 256], BF16, offset=178 * KB)
    hmT_lo = A("hmT_lo", [128, 4, 256], BF16, offset=180 * KB)
    out_stg = A("out_stg", [128, 2, D], F32, offset=182 * KB)
    h1_b = A("h1_b", [128, 4, D], F32, offset=0)

    def wdT(hc):
        return wdT_t.ap()[:, hc, :]

    pb = [nc.alloc_psum_tensor(f"pb{i}", [128, 512], F32) for i in range(8)]

    P = Prog(nc)
    bank = [Res() for _ in range(8)]

    m_ident = P.dma("sync", ident[:], ident_in[:, :])
    m_g_row = P.dma("sync", pw_f[0:1, 0, :], ln_g[:])
    m_b_row = P.dma("sync", pw_f[0:1, 1, :], ln_b[:])
    m_bvec = {}
    for w, src in (("q", bq), ("k", bk), ("v", bv)):
        m_bvec[w] = P.dma("sync", brow[w][0:1, :], src[:])
    pre_x0 = {}
    for c0 in range(2):
        pre_x0[c0] = P.dma("sync", x0_r[:, c0, :],
                           x_in[c0 * 128:(c0 + 1) * 128, :])
    m_ones = P.op("vector", lambda e: e.memset(ones_t[0:1, :], 1.0))
    m_onesb = P.op("vector", lambda e: e.memset(ones_bf[0:1, :], 1.0))

    def colize(dst, src_row, row_marker, bnk):
        mms = []
        for cc in range(8):
            rw = row_marker if isinstance(row_marker, list) else [row_marker]
            mms.append(P.op("tensor", lambda e, cc=cc, bnk=bnk, src_row=src_row:
                            e.matmul(pb[bnk][:, cc:cc + 1],
                                     src_row[0:1, cc * 128:(cc + 1) * 128],
                                     ones_t[0:1, 0:1], start=True, stop=True),
                            waits=rw + [m_ones] + (bank[bnk].free
                                                   if cc == 0 else []),
                            inc=(cc == 7)))
        mc = P.op("vector", lambda e, dst=dst, bnk=bnk: e.tensor_copy(
            out=dst, in_=pb[bnk][:, 0:8]), waits=[mms[-1]])
        bank[bnk].free = [mc]
        return mc

    m_gcol = colize(g_col[:], pw_f.ap()[:, 0, :], m_g_row, 6)
    m_bcolf = colize(bcol_f[:], pw_f.ap()[:, 1, :], m_b_row, 7)

    m_cast = {}

    def emit_casts():
        m_cast["g"] = P.dma("gpsimd", s_wg[:, :], wg_i[:, :])
        m_cast["u"] = P.dma("gpsimd", s_wu[:, :], wu_i[:, :])
        m_cast["d"] = P.dma("gpsimd", s_wd[:, :], wd_i[:, :])

    pw_free = [[m_gcol], [m_bcolf], [], []]
    ptmp_free = [[], [], [], []]
    m_wT = {}
    for wi, (w, wsrc) in enumerate((("v", Wv), ("q", Wq), ("k", Wk))):
        for rb in range(8):
            sl = (wi * 8 + rb) % 4
            rows = slice(rb * 128, (rb + 1) * 128)
            mf = P.dma("sync", pw_f[:, sl, :],
                       wsrc[rows, :], waits=pw_free[sl])
            trs = []
            for half in range(2):
                bnk = 6 + half
                mlast = None
                for j in range(4):
                    d = half * 4 + j
                    mlast = P.op("tensor",
                                 lambda e, bnk=bnk, j=j, d=d, sl=sl:
                                 e.transpose(
                                     pb[bnk][:, j * 128:(j + 1) * 128],
                                     pw_f[:, sl, d * 128:(d + 1) * 128],
                                     ident[:]),
                                 waits=[mf, m_ident] + bank[bnk].free,
                                 inc=(j == 3))
                trs.append(mlast)
            pw_free[sl] = trs[:]
            psl2 = sl % 2
            mtc0 = P.op("scalar", lambda e, psl2=psl2: e.activation(
                out=ptmp[:, psl2, 0:512], in_=pb[6][:], func=AF.Copy),
                waits=[trs[0]] + ptmp_free[psl2])
            mtc1 = P.op("vector", lambda e, psl2=psl2: e.tensor_copy(
                out=ptmp[:, psl2, 512:1024], in_=pb[7][:]),
                waits=[trs[1]] + ptmp_free[psl2])
            bank[6].free = [mtc0]
            bank[7].free = [mtc1]
            mhiw = P.op("scalar", lambda e, psl2=psl2, w=w, rb=rb: e.activation(
                out=wT[(w, "hi")].ap()[:, :, rb * 128:(rb + 1) * 128],
                in_=ptmp.ap()[:, psl2, :].rearrange("p (a b) -> p a b", b=128),
                func=AF.Copy), waits=[mtc0, mtc1])
            mlow = P.op("vector", lambda e, psl2=psl2, w=w, rb=rb: e.tensor_sub(
                out=wT[(w, "lo")].ap()[:, :, rb * 128:(rb + 1) * 128],
                in0=ptmp.ap()[:, psl2, :].rearrange("p (a b) -> p a b", b=128),
                in1=wT[(w, "hi")].ap()[:, :, rb * 128:(rb + 1) * 128]),
                waits=[mhiw])
            ptmp_free[psl2] = [mlow]
            m_wT[(w, rb)] = [mhiw, mlow]

    m_bch = P.op("scalar", lambda e: e.activation(
        out=bcol_hi[:], in_=bcol_f[:], func=AF.Copy), waits=[m_bcolf])
    m_bcl = P.op("vector", lambda e: e.tensor_sub(
        out=bcol_lo[:], in0=bcol_f[:], in1=bcol_hi[:]), waits=[m_bcolf, m_bch])
    m_az = P.op("vector", lambda e: e.memset(attn_all[:], 0.0))

    m_brow = {}
    for w in "qkv":
        wdeps = [m for rb in range(8) for m in m_wT[(w, rb)]]
        lasts = []
        for half in range(2):
            hs = slice(half * 512, (half + 1) * 512)
            n = 0
            last = None
            for d in range(8):
                for (bc, wv) in ((bcol_hi, "hi"), (bcol_lo, "hi"), (bcol_hi, "lo")):
                    n += 1
                    last = P.op("tensor", lambda e, d=d, bc=bc, wv=wv, hs=hs, w=w,
                                half=half, st=(n == 1), sp=(n == 24): e.matmul(
                        pb[half][0:1, :], bc[:, d:d + 1], wT[(w, wv)][:, d, hs],
                        start=st, stop=sp),
                        waits=([m_bch, m_bcl] + wdeps + bank[half].free)
                        if n == 1 else (),
                        inc=(n == 24))
            lasts.append(last)
        m1 = P.op("vector", lambda e, w=w: e.tensor_add(
            out=brow[w][0:1, 0:512], in0=brow[w][0:1, 0:512], in1=pb[0][0:1, :]),
            waits=[lasts[0], m_bvec[w]])
        m2 = P.op("vector", lambda e, w=w: e.tensor_add(
            out=brow[w][0:1, 512:1024], in0=brow[w][0:1, 512:1024],
            in1=pb[1][0:1, :]), waits=[lasts[1]])
        bank[0].free = [m1]
        bank[1].free = [m2]
        m_brow[w] = [m1, m2]

    m_btile = {}
    for w, (rhi, rlo) in (("v", (brow_v_hi, brow_v_lo)),):
        mh_ = P.op("scalar", lambda e, w=w, rhi=rhi: e.activation(
            out=rhi[0:1, :], in_=brow[w][0:1, :], func=AF.Copy),
            waits=m_brow[w])
        ml_ = P.op("vector", lambda e, w=w, rhi=rhi, rlo=rlo: e.tensor_sub(
            out=rlo[0:1, :], in0=brow[w][0:1, :], in1=rhi[0:1, :]),
            waits=m_brow[w] + [mh_])
        m_btile[w] = [mh_, ml_]
    mz2 = P.op("vector", lambda e: e.memset(ones2[:], 0.0))
    mo2a = P.op("vector", lambda e: e.memset(ones2[0:64, 0:1], 1.0), waits=[mz2])
    m_ones2 = P.op("vector", lambda e: e.memset(ones2[64:128, 1:2], 1.0),
                   waits=[mo2a])

    m_bkc = colize(bk_col[:], brow["k"], m_brow["k"], 6)
    m_bqc = colize(bq_col[:], brow["q"], m_brow["q"], 7)


    v_ready = ([m for rb in range(8) for m in m_wT[("v", rb)]]
               + m_btile["v"] + [m_gcol, m_ident, m_onesb])
    qk_ready = ([m for w in "qk" for rb in range(8) for m in m_wT[(w, rb)]]
                + [m_bkc, m_bqc, m_ones2])

    x0_res = [Res(), Res()]
    xn_res = [Res(), Res()]
    r_res = [Res(), Res()]
    q_res = [Res(), Res()]
    eq_res = [Res(), Res()]
    qTs_res = [Res(), Res()]
    kTs_res = [Res(), Res()]
    vs_res = [Res(), Res()]
    xnT_res = [Res() for _ in range(4)]
    n1_free = []
    m_ph1end = {}
    m_negMk = {}
    m_kstore = {}
    m_vstore = {}
    m_qTstore = {}
    m_rstore = {}
    m_h1store = {}
    m_hnT_store = []
    m_attn = {}

    def ln_stats(src_ap, dep_markers, mean_slot, rstd_slot):
        bnv = stat.ap()[:, 32:44].rearrange("p (g x) -> p g x", x=6)
        mb1 = P.op("vector", lambda e: e.bn_stats(
            out=stat.ap()[:, 32:38], in_=src_ap[:, 0:512]), waits=dep_markers)
        mb2 = P.op("vector", lambda e: e.bn_stats(
            out=stat.ap()[:, 38:44], in_=src_ap[:, 512:1024]), waits=dep_markers)
        mag = P.op("vector", lambda e, ms=mean_slot: e.bn_aggr(
            out=stat[:, ms:ms + 2], in_=bnv), waits=[mb1, mb2])
        mvp = P.op("vector", lambda e, ms=mean_slot: e.tensor_scalar_add(
            out=stat[:, ms + 2:ms + 3], in0=stat[:, ms + 1:ms + 2],
            scalar1=EPS), waits=[mag])
        msr = P.op("scalar", lambda e, ms=mean_slot: e.activation(
            out=stat[:, ms + 3:ms + 4], in_=stat[:, ms + 2:ms + 3],
            func=AF.Sqrt), waits=[mvp])
        mrc = P.op("vector", lambda e, ms=mean_slot, rs=rstd_slot: e.reciprocal(
            out=stat[:, rs:rs + 1], in_=stat[:, ms + 3:ms + 4]), waits=[msr])
        return mrc

    def phase1(s):
        nonlocal n1_free
        m_mkinit = P.op("vector", lambda e: e.memset(Mk[:], -3.0e38))

        def S1(c):
            nonlocal n1_free
            tok0 = s * T + c * 128
            sl = c % 2
            x0 = x0_r[:, sl, :]
            if s == 0 and c < 2:
                mx = pre_x0[c]
            else:
                mx = P.dma("sync", x0, x_in[tok0:tok0 + 128, :],
                           waits=x0_res[sl].free)
            mrc1 = ln_stats(x0, [mx] + n1_free, 0, 4)
            mnb1 = P.op("vector", lambda e: e.tensor_scalar(
                out=stat[:, 10:11], in0=stat[:, 0:1], scalar1=stat[:, 4:5],
                scalar2=-1.0, op0=OP.mult, op1=OP.mult), waits=[mrc1])
            mn1 = P.op("scalar", lambda e, x0=x0: e.activation(
                out=n1_t[:], in_=x0, func=AF.Identity,
                bias=stat[:, 10:11], scale=stat[:, 4:5]),
                waits=[mnb1] + n1_free)
            mr = P.op("vector", lambda e, x0=x0, sl=sl: e.tensor_add(
                out=r_r[:, sl, :], in0=x0, in1=n1_t[:]),
                waits=[mn1] + r_res[sl].free)
            x0_res[sl].free = [mr]
            mrst = P.dma("gpsimd", s_r[s, c * 128:(c + 1) * 128, :],
                         r_r[:, sl, :], waits=[mr])
            r_res[sl].free = [mrst]
            m_rstore[(s, c)] = mrst
            mrc2 = ln_stats(n1_t.ap(), [mn1], 5, 9)
            mnb2 = P.op("vector", lambda e: e.tensor_scalar(
                out=stat[:, 11:12], in0=stat[:, 5:6], scalar1=stat[:, 9:10],
                scalar2=-1.0, op0=OP.mult, op1=OP.mult), waits=[mrc2])
            mxn = P.op("scalar", lambda e, sl=sl: e.activation(
                out=xn_r[:, sl, :], in_=n1_t[:], func=AF.Identity,
                bias=stat[:, 11:12], scale=stat[:, 9:10]),
                waits=[mnb2] + xn_res[sl].free)
            xn_res[sl].ready = [mxn]
            return {"mxn": mxn}

        def S2(c, st1):
            nonlocal n1_free
            sl = c % 2
            tsl = c % 4
            mxn = st1["mxn"]
            tms = []
            for half in range(2):
                bnk = half
                mlast = None
                for j in range(4):
                    d = half * 4 + j
                    mlast = P.op("tensor",
                                 lambda e, bnk=bnk, j=j, d=d, sl=sl:
                                 e.transpose(
                                     pb[bnk][:, j * 128:(j + 1) * 128],
                                     xn_r[:, sl, d * 128:(d + 1) * 128],
                                     ident[:]),
                                 waits=[mxn, m_ident] + bank[bnk].free
                                 + xnT_res[tsl].free,
                                 inc=(j == 3))
                tms.append(mlast)
            xn_res[sl].free = tms[:]
            evac0 = []
            for j in range(4):
                d = j
                evac0.append(P.op("scalar", lambda e, j=j, d=d: e.activation(
                    out=n1_t[:, d * 128:(d + 1) * 128],
                    in_=pb[0][:, j * 128:(j + 1) * 128],
                    func=AF.Copy, scale=g_col[:, d:d + 1]),
                    waits=[tms[0], m_gcol]))
            evac1 = []
            for j in range(4):
                d = 4 + j
                evac1.append(P.op("vector",
                                  lambda e, j=j, d=d: e.tensor_scalar_mul(
                                      out=n1_t[:, d * 128:(d + 1) * 128],
                                      in0=pb[1][:, j * 128:(j + 1) * 128],
                                      scalar1=g_col[:, d:d + 1]),
                                  waits=[tms[1]]))
            mhi = P.op("scalar", lambda e, tsl=tsl: e.activation(
                out=xnT_hi.ap()[:, :, tsl, :], in_=n1_t[:], func=AF.Copy),
                waits=evac0 + evac1)
            mlo = P.op("vector", lambda e, tsl=tsl: e.tensor_sub(
                out=xnT_lo.ap()[:, :, tsl, :], in0=n1_t[:],
                in1=xnT_hi.ap()[:, :, tsl, :]), waits=[mhi])
            evs = [mhi, mlo]
            bank[0].free = [mhi] + evac0
            bank[1].free = [mhi, mlo] + evac1
            xnT_res[tsl].ready = evs
            n1_free = [mlo]
            return {"evs": evs}

        def S3(c, st2):
            tsl = c % 4
            n = 0
            mhi_m, mlo_m = st2["evs"]
            w0 = (v_ready + [mhi_m] + bank[4].free + bank[5].free)
            for (xt, wver, movs) in (
                (xnT_hi, "hi", (("v", "hi"), ("v", "lo"))),
                (xnT_lo, "lo", (("v", "hi"),)),
            ):
                for d in range(8):
                    for (w, mv) in movs:
                        for half in range(2):
                            n += 1
                            bnk = 4 + half
                            st = (d == 0 and wver == "hi" and mv == "hi")
                            P.op("tensor",
                                 lambda e, bnk=bnk, d=d, xt=xt, tsl=tsl,
                                 mv=mv, half=half, st=st: e.matmul(
                                     pb[bnk][:], xt[:, d, tsl, :],
                                     wT[("v", mv)][:, d,
                                                   half * 512:(half + 1) * 512],
                                     start=st, stop=False),
                                 waits=w0 if n == 1
                                 else ([mlo_m] if (wver == "lo" and d == 0
                                                   and half == 0) else ()),
                                 inc=False)
            mqv = None
            for half in range(2):
                hs = slice(half * 512, (half + 1) * 512)
                bnk = 4 + half
                P.op("tensor", lambda e, bnk=bnk, hs=hs:
                     e.matmul(pb[bnk][:], ones_bf[0:1, 0:128],
                              brow_v_hi[0:1, hs], start=False, stop=False),
                     inc=False)
                mqv = P.op("tensor", lambda e, bnk=bnk, hs=hs:
                           e.matmul(pb[bnk][:], ones_bf[0:1, 0:128],
                                    brow_v_lo[0:1, hs], start=False,
                                    stop=True),
                           inc=(half == 1))
            return {"mqv": mqv}

        def S4(c, st3):
            mqv = st3["mqv"]
            qsl = c % 2
            mve1 = P.op("scalar", lambda e, qsl=qsl: e.activation(
                out=v_stg[:, qsl, 0:512], in_=pb[4][:], func=AF.Copy),
                waits=[mqv] + vs_res[qsl].free)
            mve2 = P.op("vector", lambda e, qsl=qsl: e.tensor_copy(
                out=v_stg[:, qsl, 512:1024], in_=pb[5][:]),
                waits=[mqv] + vs_res[qsl].free)
            bank[4].free = [mve1]
            bank[5].free = [mve2]
            mvs = P.dma("gpsimd", s_v[s, c * 128:(c + 1) * 128, :],
                        v_stg[:, qsl, :], waits=[mve1, mve2])
            vs_res[qsl].free = [mvs]
            m_vstore[(s, c)] = mvs

        def kblock(blk):
            ktok = slice(blk * 512, (blk + 1) * 512)
            kdeps = [m for t4 in range(4) for m in xnT_res[t4].ready] \
                + (qk_ready if blk == 0 else [])
            klast = None
            for e_ in range(8):
                for (wname, bbase, res, stg, stores, mcol) in (
                    ("q", 2, qTs_res, qT_stg, m_qTstore, m_bqc),
                    ("k", 6, kTs_res, kT_stg, m_kstore, m_bkc),
                ):
                    bnk = bbase + (e_ % 2)
                    n = 0
                    last = None
                    for d in range(8):
                        for (wver, xt) in (("hi", xnT_hi), ("hi", xnT_lo),
                                           ("lo", xnT_hi)):
                            n += 1
                            st = (n == 1)
                            sp = (n == 24)
                            last = P.op("tensor",
                                        lambda e, bnk=bnk, d=d, e_=e_, xt=xt,
                                        wver=wver, st=st, sp=sp, wname=wname:
                                        e.matmul(
                                            pb[bnk][:],
                                            wT[(wname, wver)][
                                                :, d, e_ * 128:(e_ + 1) * 128],
                                            xt.ap()[:, d, :, :].rearrange(
                                                "p a b -> p (a b)"),
                                            start=st, stop=sp),
                                        waits=(kdeps + bank[bnk].free)
                                        if n == 1 else (),
                                        inc=(n == 24))
                    klast = last
                    ksl = e_ % 2
                    if wname == "q":
                        mke = P.op("scalar",
                                   lambda e, bnk=bnk, ksl=ksl, e_=e_:
                                   e.activation(
                                       out=qT_stg[:, ksl, :], in_=pb[bnk][:],
                                       func=AF.Exp,
                                       bias=bq_col[:, e_:e_ + 1]),
                                   waits=[last, m_bqc] + qTs_res[ksl].free)
                        bank[bnk].free = [mke]
                        mks = P.dma("gpsimd", s_qT[s, e_, :, ktok],
                                    qT_stg[:, ksl, :], waits=[mke])
                        qTs_res[ksl].free = [mks]
                        m_qTstore[(s, e_, blk)] = mks
                    else:
                        mke = P.op("scalar",
                                   lambda e, bnk=bnk, ksl=ksl, e_=e_:
                                   e.activation(
                                       out=kT_stg[:, ksl, :], in_=pb[bnk][:],
                                       func=AF.Identity,
                                       bias=bk_col[:, e_:e_ + 1]),
                                   waits=[last, m_bkc] + kTs_res[ksl].free)
                        bank[bnk].free = [mke]
                        mkm = P.op("vector", lambda e, ksl=ksl: e.tensor_reduce(
                            out=stat[:, 14:15], in_=kT_stg[:, ksl, :],
                            axis=AX.X, op=OP.max), waits=[mke])
                        P.op("vector", lambda e, e_=e_: e.tensor_max(
                            out=Mk[:, e_:e_ + 1], in0=Mk[:, e_:e_ + 1],
                            in1=stat[:, 14:15]), waits=[mkm, m_mkinit])
                        mks = P.dma("gpsimd", s_kT[s, e_, :, ktok],
                                    kT_stg[:, ksl, :], waits=[mke])
                        kTs_res[ksl].free = [mks, mkm]
                        m_kstore[(s, e_, blk)] = mks
            for t4 in range(4):
                xnT_res[t4].free = [klast]

        nch = min(nchunks, CPS)
        st1 = {0: S1(0)} if nch > 0 else {}
        st2 = {}
        st3 = {}
        for c in range(nch):
            st2[c] = S2(c, st1[c])
            if c - 1 >= 0:
                S4(c - 1, st3[c - 1])
            if c + 1 < nch:
                st1[c + 1] = S1(c + 1)
            st3[c] = S3(c, st2[c])
            if c % 4 == 3 and (c + 1) * 128 <= T:
                kblock(c // 4)
        for c in range(max(nch - 1, 0), nch):
            S4(c, st3[c])
        if nch >= CPS:
            m_negMk[s] = P.op("vector", lambda e: e.tensor_scalar_mul(
                out=nMk[:], in0=Mk[:], scalar1=-1.0))
        m_ph1end[s] = P.cur("te")

    ph2_state = {}

    def phase2_prefetch(s):
        ek_res = [Res() for _ in range(2)]
        kTe_res = [Res(), Res()]
        ekq_res = [Res(), Res()]
        vp_res = [Res(), Res()]
        loads = {}

        def load_pair(p_):
            psl = p_ % 2
            mload = P.dma("sync", kTe2[:, psl, :], s_kT[s, p_, :, :],
                          waits=[m_kstore[(s, p_, blk)] for blk in range(4)]
                          + kTe_res[psl].free)
            mvp = P.dma("sync", v_p2[:, psl, :, :],
                        s_v[s].rearrange("(c p) j -> p c j", p=128)[
                            :, :, p_ * 128:(p_ + 1) * 128],
                        waits=[m_vstore[(s, c)] for c in range(CPS)]
                        + [m_rstore[(s, c)] for c in (CPS - 2, CPS - 1)]
                        + vp_res[psl].free)
            loads[p_] = (mload, mvp)

        ph2_state[s] = (ek_res, kTe_res, ekq_res, vp_res, loads, load_pair)
        load_pair(0)
        load_pair(1)

    def phase2(s):
        (ek_res, kTe_res, ekq_res, vp_res, loads, load_pair) = ph2_state[s]
        mneg = m_negMk[s]
        for p_ in range(8):
            psl = p_ % 2
            mload, mvp = loads[p_]
            abnk = 2 + (p_ % 2)
            last_mm = None
            exp_marks = []
            msk_last = None
            trs_q = {}

            def emit_exp_tr(qt):
                nonlocal msk_last
                esl2 = qt % 2
                mexp = P.op("scalar", lambda e, p_=p_, psl=psl, qt=qt,
                            esl2=esl2: e.activation(
                                out=ekTq[:, esl2, :],
                                in_=kTe2[:, psl, qt * 512:(qt + 1) * 512],
                                func=AF.Exp, bias=nMk[:, p_:p_ + 1]),
                            waits=[mload, mneg] + ekq_res[esl2].free)
                exp_marks.append(mexp)
                if qt == 0:
                    msk_last = P.op("vector",
                                    lambda e, esl2=esl2: e.tensor_reduce(
                                        out=stat[:, 15:16],
                                        in_=ekTq[:, esl2, :], axis=AX.X,
                                        op=OP.add), waits=[mexp])
                else:
                    mr1 = P.op("vector", lambda e, esl2=esl2: e.tensor_reduce(
                        out=stat[:, 13:14], in_=ekTq[:, esl2, :], axis=AX.X,
                        op=OP.add), waits=[mexp])
                    msk_last = P.op("vector", lambda e: e.tensor_add(
                        out=stat[:, 15:16], in0=stat[:, 15:16],
                        in1=stat[:, 13:14]), waits=[mr1])
                btr = qt % 2
                last_tr = None
                for t4 in range(4):
                    last_tr = P.op("tensor", lambda e, t4=t4, btr=btr,
                                   esl2=esl2: e.transpose(
                                       pb[btr][:, t4 * 128:(t4 + 1) * 128],
                                       ekTq[:, esl2, t4 * 128:(t4 + 1) * 128],
                                       ident[:]),
                                   waits=([mexp, m_ident] + bank[btr].free
                                          + ek_res[qt % 2].free)
                                   if t4 == 0 else [mexp],
                                   inc=(t4 == 3))
                trs_q[qt] = last_tr
                ekq_res[esl2].free = [last_tr, msk_last]

            emit_exp_tr(0)
            for qt in range(4):
                if qt + 1 < 4:
                    emit_exp_tr(qt + 1)
                btr = qt % 2
                mev = P.op("vector", lambda e, btr=btr, qt=qt: e.tensor_copy(
                    out=ek_r.ap()[:, qt % 2, :, :].rearrange(
                        "p a b -> p (a b)"),
                    in_=pb[btr][:]), waits=[trs_q[qt]] + ek_res[qt % 2].free)
                bank[btr].free = [mev]
                for t4 in range(4):
                    t = qt * 4 + t4
                    mmm = P.op("tensor", lambda e, t=t, t4=t4, qt=qt, psl=psl,
                               abnk=abnk: e.matmul(
                                   pb[abnk][:, 0:128], ek_r[:, qt % 2, t4, :],
                                   v_p2[:, psl, t, :],
                                   start=(t == 0), stop=(t == 15)),
                               waits=[mev, mvp]
                               + (bank[abnk].free if t == 0 else []),
                               inc=(t4 == 3))
                    last_mm = mmm
                ek_res[qt % 2].free = [last_mm]
            kTe_res[psl].free = [exp_marks[-1]]
            vp_res[psl].free = [last_mm]
            if p_ + 2 < 8:
                load_pair(p_ + 2)
            mrk = P.op("vector", lambda e, p_=p_: e.reciprocal(
                out=rk[:, p_:p_ + 1], in_=stat[:, 15:16]), waits=[msk_last])
            ma1 = P.op("vector", lambda e, p_=p_, abnk=abnk: e.tensor_scalar_mul(
                out=attn_all[0:64, p_, 0:64], in0=pb[abnk][0:64, 0:64],
                scalar1=rk[0:64, p_:p_ + 1]), waits=[last_mm, mrk, m_az])
            ma2 = P.op("vector", lambda e, p_=p_, abnk=abnk: e.tensor_scalar_mul(
                out=attn_all[64:128, p_, 64:128], in0=pb[abnk][64:128, 64:128],
                scalar1=rk[64:128, p_:p_ + 1]), waits=[last_mm, mrk])
            bank[abnk].free = [ma1, ma2]
            m_attn[(s, p_)] = [ma1, ma2]

    def phase3(s):
        qc_res = [Res(), Res()]
        rc_res = [Res(), Res()]
        h1_res = [Res(), Res()]
        hnst_free = []
        hn_prev_act = []
        attn_deps = [m for p_ in range(8) for m in m_attn[(s, p_)]]
        pending_hn = [None]

        def flush_hn():
            if pending_hn[0] is not None:
                pending_hn[0]()
                pending_hn[0] = None

        def make_hn_flush(c, mhn):
            hsl = c % 2
            csl = c % 2

            def emit():
                nonlocal hnst_free, hn_prev_act
                ths = []
                for half in range(2):
                    bnk = half
                    mlast = None
                    for j in range(4):
                        d = half * 4 + j
                        mlast = P.op("tensor",
                                     lambda e, bnk=bnk, j=j, d=d, hsl=hsl:
                                     e.transpose(
                                         pb[bnk][:, j * 128:(j + 1) * 128],
                                         h1n_t[:, hsl, d * 128:(d + 1) * 128],
                                         ident[:]),
                                     waits=[mhn] + bank[bnk].free
                                     + (hnst_free if c % 2 == 0 else []),
                                     inc=(j == 3))
                    ths.append(mlast)
                mc0 = P.op("scalar", lambda e, hsl=hsl: e.activation(
                    out=h1n_t[:, hsl, 0:512], in_=pb[0][:], func=AF.Copy),
                    waits=[ths[0]])
                mc1 = P.op("vector", lambda e, hsl=hsl: e.tensor_copy(
                    out=h1n_t[:, hsl, 512:1024], in_=pb[1][:]), waits=[ths[1]])
                mhi3 = P.op("scalar", lambda e, csl=csl, hsl=hsl: e.activation(
                    out=hnT_hi_st.ap()[:, :, csl * 128:(csl + 1) * 128],
                    in_=h1n_t.ap()[:, hsl, :].rearrange("p (a b) -> p a b",
                                                        b=128),
                    func=AF.Copy), waits=[mc0, mc1])
                mlo3 = P.op("vector", lambda e, csl=csl, hsl=hsl: e.tensor_sub(
                    out=hnT_lo_st.ap()[:, :, csl * 128:(csl + 1) * 128],
                    in0=h1n_t.ap()[:, hsl, :].rearrange("p (a b) -> p a b",
                                                        b=128),
                    in1=hnT_hi_st.ap()[:, :, csl * 128:(csl + 1) * 128]),
                    waits=[mhi3])
                bank[0].free = [mc0]
                bank[1].free = [mc1]
                hn_prev_act = [mhi3, mlo3]
                if c % 2 == 1:
                    tb = s * T + (c - 1) * 128
                    mh_s = P.dma("gpsimd",
                                 s_hnT_hi[:, :, tb:tb + 256].transpose(
                                     [1, 0, 2]),
                                 hnT_hi_st[:], waits=[mhi3, mlo3])
                    ml_s = P.dma("gpsimd",
                                 s_hnT_lo[:, :, tb:tb + 256].transpose(
                                     [1, 0, 2]),
                                 hnT_lo_st[:], waits=[mhi3, mlo3])
                    hnst_free = [mh_s, ml_s]
                    m_hnT_store.append([mh_s, ml_s])
            return emit

        for c in range(CPS):
            sl = c % 2
            tok0 = s * T + c * 128
            mq = P.dma("sync", qT_c[:, sl, :, :],
                       s_qT[s, :, :, c * 128:(c + 1) * 128].transpose(
                           [1, 0, 2]),
                       waits=[m_qTstore[(s, e_, c // 4)] for e_ in range(8)]
                       + qc_res[sl].free)
            mr = P.dma("sync", r_c[:, sl, :], s_r[s, c * 128:(c + 1) * 128, :],
                       waits=[m_rstore[(s, c)]] + rc_res[sl].free)
            b0 = 4 if sl == 0 else 6
            sqm = None
            for p_ in range(8):
                sqm = P.op("tensor", lambda e, p_=p_, sl=sl: e.matmul(
                    pb[2][:, p_ * 2:(p_ + 1) * 2], qT_c[:, sl, p_, :],
                    ones2[:, :], start=True, stop=True),
                    waits=([mq] + bank[2].free) if p_ == 0 else (),
                    inc=(p_ == 7))
            msqe = P.op("vector", lambda e: e.tensor_copy(
                out=stat[:, 16:32], in_=pb[2][:, 0:16]), waits=[sqm])
            bank[2].free = [msqe]
            mrq = P.op("vector", lambda e, c=c: e.reciprocal(
                out=rq_all[:, c, :], in_=stat[:, 16:32]), waits=[msqe])
            yms = []
            for p_ in range(8):
                bnk = b0 + p_ // 4
                mmy = P.op("tensor", lambda e, p_=p_, bnk=bnk, sl=sl: e.matmul(
                    pb[bnk][:, (p_ % 4) * 128:(p_ % 4 + 1) * 128],
                    qT_c[:, sl, p_, :], attn_all[:, p_, :],
                    start=True, stop=True),
                    waits=[mq] + attn_deps
                    + (bank[bnk].free if p_ % 4 == 0 else []),
                    inc=(p_ % 4 == 3))
                if p_ % 4 == 3:
                    yms.append(mmy)
            flush_hn()
            hevs = []
            for p_ in range(8):
                bnk = b0 + p_ // 4
                for hh in range(2):
                    head = 2 * p_ + hh
                    cs = slice(p_ * 128 + hh * 64, p_ * 128 + (hh + 1) * 64)
                    ps = slice((p_ % 4) * 128 + hh * 64,
                               (p_ % 4) * 128 + (hh + 1) * 64)
                    mh = P.op("vector",
                              lambda e, bnk=bnk, cs=cs, ps=ps, head=head, c=c,
                              sl=sl: e.scalar_tensor_tensor(
                                  out=h1_st[:, sl, cs], in0=pb[bnk].ap()[:, ps],
                                  scalar=rq_all[:, c, head:head + 1],
                                  in1=r_c[:, sl, cs], op0=OP.mult, op1=OP.add),
                              waits=[yms[p_ // 4], mr, mrq]
                              + h1_res[sl].free)
                    hevs.append(mh)
            bank[b0].free = hevs[:8]
            bank[b0 + 1].free = hevs[8:]
            rc_res[sl].free = [hevs[-1]]
            qc_res[sl].free = yms[:] + [sqm]
            mh1 = P.dma("gpsimd", s_h1[tok0:tok0 + 128, :], h1_st[:, sl, :],
                        waits=hevs)
            m_h1store[(s, c)] = mh1
            h1_res[sl].free = [mh1]

            mrc3 = ln_stats(h1_st.ap()[:, sl, :], hevs, 0, 4)
            mnb3 = P.op("vector", lambda e: e.tensor_scalar(
                out=stat[:, 10:11], in0=stat[:, 0:1], scalar1=stat[:, 4:5],
                scalar2=-1.0, op0=OP.mult, op1=OP.mult), waits=[mrc3])
            mhn = P.op("scalar", lambda e, sl=sl, c=c: e.activation(
                out=h1n_t[:, c % 2, :], in_=h1_st[:, sl, :], func=AF.Identity,
                bias=stat[:, 10:11], scale=stat[:, 4:5]),
                waits=[mnb3] + hn_prev_act)
            pending_hn[0] = make_hn_flush(c, mhn)
        flush_hn()

    stages = []
    for s in range(NSEQ):
        stages += [(f"p1s{s}", phase1), (f"p2s{s}", phase2), (f"p3s{s}", phase3)]
    early_ffn = {"m": None}

    def emit_early_ffn_loads():
        deps = [m_ph1end[1]]
        mw = {}
        for d in range(8):
            for half in range(2):
                rows = slice(half * 1536, (half + 1) * 1536)
                mw[("g", d, half)] = P.dma(
                    "scalar", wgT[:, d, rows], s_wg[rows, d * 128:(d + 1) * 128],
                    waits=deps + [m_cast["g"]], transpose=True)
                mw[("u", d, half)] = P.dma(
                    "scalar", wuT[:, d, rows], s_wu[rows, d * 128:(d + 1) * 128],
                    waits=[m_cast["u"]] + deps, transpose=True)
        early_ffn["m"] = mw

    ffn_pre = {}

    def prefetch_ffn_blocks():
        (ek_res2, kTe_res2, ekq_res2, vp_res2, _, _) = ph2_state[1]
        ph2done = (vp_res2[0].free + vp_res2[1].free
                   + kTe_res2[0].free + kTe_res2[1].free)
        for b in range(2):
            t0b = b * 256
            mhi = P.dma("sync", hnT_hi_b[:, b % 2, :, :],
                        s_hnT_hi[:, :, t0b:t0b + 256].transpose([1, 0, 2]),
                        waits=m_hnT_store[b] + ph2done)
            mlo = P.dma("sync", hnT_lo_b[:, b % 2, :, :],
                        s_hnT_lo[:, :, t0b:t0b + 256].transpose([1, 0, 2]),
                        waits=m_hnT_store[b] + ph2done)
            ffn_pre[b] = [mhi, mlo]

    if upto != "all":
        emit_casts()
    done = upto == "prologue"
    for nm, fn in stages:
        if done:
            break
        fn(int(nm[-1]))
        if nm.startswith("p1") and upto == "all":
            pre = P.fence_markers()
            phase2_prefetch(int(nm[-1]))
            if nm == "p1s0":
                emit_casts()
            P.fence(pre)
        elif nm == "p2s1" and upto == "all":
            pre = P.fence_markers()
            emit_early_ffn_loads()
            prefetch_ffn_blocks()
            P.fence(pre)
        elif nm.startswith("p2"):
            pass
        else:
            P.fence()
        if nm == upto:
            done = True

    if done:
        P.op("gpsimd", lambda e: e.nop(),
             waits=P.queue_tail("gpsimd") + P.queue_tail("sync"), inc=False)
        with nc.Block() as block:
            P.emit(block)
        return nc
    mw = early_ffn["m"]
    m_wg = {(d, h): mw[("g", d, h)] for d in range(8) for h in range(2)}
    m_wu = {(d, h): mw[("u", d, h)] for d in range(8) for h in range(2)}
    m_wd = {}
    for hc in range(24):
        m_wd[hc] = P.dma(
            "sync", wdT_t[:, hc, :], s_wd[0:D, hc * 128:(hc + 1) * 128],
            waits=[m_cast["d"]], transpose=True)

    hb_res = [Res(), Res()]
    h1b_res = [Res() for _ in range(4)]
    g1_res = [Res(), Res()]
    hm_res = [Res(), Res()]
    out_res = [Res(), Res()]

    NBLK = TOK // 256
    hm_res4 = [Res() for _ in range(4)]
    for b in range(NBLK):
        bsl = b % 2
        t0 = b * 256
        if b in ffn_pre:
            mhi, mlo = ffn_pre[b]
        else:
            mhi = P.dma("sync", hnT_hi_b[:, bsl, :, :],
                        s_hnT_hi[:, :, t0:t0 + 256].transpose([1, 0, 2]),
                        waits=m_hnT_store[b] + hb_res[bsl].free)
            mlo = P.dma("sync", hnT_lo_b[:, bsl, :, :],
                        s_hnT_lo[:, :, t0:t0 + 256].transpose([1, 0, 2]),
                        waits=m_hnT_store[b] + hb_res[bsl].free)
        mh1l = []
        for tc in range(2):
            hsl = (2 * b + tc) % 4
            sseq = t0 // T
            cch = (t0 % T) // 128 + tc
            ml = P.dma("sync", h1_b[:, hsl, :],
                       s_h1[t0 + tc * 128:t0 + (tc + 1) * 128, :],
                       waits=[m_h1store[(sseq, cch)]] + h1b_res[hsl].free)
            mh1l.append(ml)

        down_banks = [4, 5] if bsl == 0 else [6, 7]
        dn_last = {}
        pending_dn = [None]

        def flush_dn():
            if pending_dn[0] is not None:
                pending_dn[0]()
                pending_dn[0] = None

        def make_dn(hc, hsl4, mhh, mhl, down_banks=down_banks, dn_last=dn_last):
            def emit():
                dlast = None
                for tc in range(2):
                    for (hmt, mdep) in ((hmT_hi, mhh), (hmT_lo, mhl)):
                        for half in range(2):
                            bnk = (down_banks if tc == 0
                                   else [down_banks[0] ^ 2,
                                         down_banks[1] ^ 2])[half]
                            st = (hc == 0 and hmt is hmT_hi)
                            sp = (hc == 23 and hmt is hmT_lo)
                            dlast = P.op(
                                "tensor",
                                lambda e, hmt=hmt, hsl4=hsl4, tc=tc, hc=hc,
                                half=half, bnk=bnk, st=st, sp=sp: e.matmul(
                                    pb[bnk][:],
                                    hmt[:, hsl4, tc * 128:(tc + 1) * 128],
                                    wdT(hc)[:, half * 512:(half + 1) * 512],
                                    start=st, stop=sp),
                                waits=[mdep, m_wd[hc]]
                                + (bank[bnk].free if hc == 0 else []),
                                inc=(hc == 23))
                            if hc == 23:
                                dn_last[(tc, half)] = dlast
                hm_res4[hsl4].free = [dlast]
            return emit

        for hc in range(24):
            gsl = hc % 2
            hsl4 = hc % 4
            gb = gsl
            ub = 2 + gsl
            wdeps = ([m_wg[(d, h)] for d in range(8) for h in range(2)]
                     + [m_wu[(d, h)] for d in range(8) for h in range(2)]) \
                if (b == 0 and hc == 0) else []
            n = 0
            mgu = None
            for d in range(8):
                for wtt, bnk in ((wgT, gb), (wuT, ub)):
                    for xt in (hnT_hi_b, hnT_lo_b):
                        n += 1
                        st = (d == 0 and xt is hnT_hi_b)
                        sp = (d == 7 and xt is hnT_lo_b)
                        mgu = P.op("tensor",
                                   lambda e, d=d, wtt=wtt, bnk=bnk, xt=xt,
                                   hc=hc, bsl=bsl, st=st, sp=sp: e.matmul(
                                       pb[bnk][:, 0:256],
                                       wtt[:, d, hc * 128:(hc + 1) * 128],
                                       xt.ap()[:, bsl, d, :],
                                       start=st, stop=sp),
                                   waits=(wdeps + [mhi, mlo] + bank[gb].free
                                          + bank[ub].free) if n == 1 else (),
                                   inc=(n == 32))
            flush_dn()
            mg1 = P.op("scalar", lambda e, gb=gb, gsl=gsl: e.activation(
                out=g1_t[:, gsl, :], in_=pb[gb][:, 0:256], func=AF.Silu,
                scale=QS), waits=[mgu] + g1_res[gsl].free)
            mhm = P.op("vector",
                       lambda e, ub=ub, gsl=gsl: e.scalar_tensor_tensor(
                           out=g1_t[:, gsl, :], in0=pb[ub][:, 0:256], scalar=QS,
                           in1=g1_t[:, gsl, :], op0=OP.mult, op1=OP.mult),
                       waits=[mg1])
            bank[gb].free = [mhm]
            bank[ub].free = [mhm]
            mhh = P.op("vector", lambda e, gsl=gsl, hsl4=hsl4: e.tensor_copy(
                out=hmT_hi[:, hsl4, :], in_=g1_t[:, gsl, :]),
                waits=[mhm] + hm_res4[hsl4].free)
            mhl = P.op("vector", lambda e, gsl=gsl, hsl4=hsl4: e.tensor_sub(
                out=hmT_lo[:, hsl4, :], in0=g1_t[:, gsl, :],
                in1=hmT_hi[:, hsl4, :]), waits=[mhh])
            g1_res[gsl].free = [mhl]
            pending_dn[0] = make_dn(hc, hsl4, mhh, mhl)
        flush_dn()

        for tc in range(2):
            osl = tc
            hsl = (2 * b + tc) % 4
            oevs = []
            for half in range(2):
                bnk = (down_banks if tc == 0
                       else [down_banks[0] ^ 2, down_banks[1] ^ 2])[half]
                mo = P.op("vector",
                          lambda e, bnk=bnk, osl=osl, hsl=hsl, half=half:
                          e.scalar_tensor_tensor(
                              out=out_stg[:, osl, half * 512:(half + 1) * 512],
                              in0=pb[bnk][:], scalar=QS,
                              in1=h1_b.ap()[:, hsl,
                                            half * 512:(half + 1) * 512],
                              op0=OP.mult, op1=OP.add),
                          waits=[dn_last[(tc, half)], mh1l[tc]]
                          + out_res[osl].free)
                bank[bnk].free = [mo]
                oevs.append(mo)
            most = P.dma("gpsimd",
                         out_hbm[t0 + tc * 128:t0 + (tc + 1) * 128, :],
                         out_stg[:, osl, :], waits=oevs)
            out_res[osl].free = [most]
            h1b_res[hsl].free = [most]
        hb_res[bsl].free = [dn_last[(1, 1)]]

    P.op("gpsimd", lambda e: e.nop(),
         waits=P.queue_tail("gpsimd") + P.queue_tail("sync")
         + P.queue_tail("scalar"), inc=False)

    with nc.Block() as block:
        P.emit(block)
    return nc


_NC_CACHE = {}


def kernel(**inputs):
    x = np.ascontiguousarray(np.asarray(inputs["L0_attn_x"], dtype=np.float32))
    B = x.shape[0]
    per = B // NCORES
    key = "main"
    if key not in _NC_CACHE:
        _NC_CACHE[key] = build()
    nc = _NC_CACHE[key]
    common = {
        "ln_g": np.asarray(inputs["L0_attn_ln_g"], dtype=np.float32),
        "ln_b": np.asarray(inputs["L0_attn_ln_b"], dtype=np.float32),
        "Wq": np.asarray(inputs["L0_attn_Wq"], dtype=np.float32),
        "Wk": np.asarray(inputs["L0_attn_Wk"], dtype=np.float32),
        "Wv": np.asarray(inputs["L0_attn_Wv"], dtype=np.float32),
        "bq": np.asarray(inputs["L0_attn_bq"], dtype=np.float32),
        "bk": np.asarray(inputs["L0_attn_bk"], dtype=np.float32),
        "bv": np.asarray(inputs["L0_attn_bv"], dtype=np.float32),
        "wg": np.asarray(inputs["L0_ffn_w_gate"], dtype=np.int32),
        "wu": np.asarray(inputs["L0_ffn_w_up"], dtype=np.int32),
        "wd": np.asarray(inputs["L0_ffn_w_down"], dtype=np.int32),
        "ident": np.eye(128, dtype=np.float32),
    }
    in_maps = []
    for c in range(NCORES):
        shard = np.ascontiguousarray(
            x[c * per:(c + 1) * per].reshape(per * x.shape[1], x.shape[2]))
        m = dict(common)
        m["x"] = shard
        in_maps.append(m)
    res = run_bass_kernel_spmd(nc, in_maps, core_ids=list(range(NCORES)))
    out = np.concatenate([np.asarray(r["out"]) for r in res.results], axis=0)
    return out.reshape(B, x.shape[1], x.shape[2])


# version 17
# speedup vs baseline: 1.0153x; 1.0153x over previous
import numpy as np
import concourse.bass as bass
import concourse.mybir as mybir
from concourse.bass_utils import run_bass_kernel_spmd

F32 = mybir.dt.float32
BF16 = mybir.dt.bfloat16
I32 = mybir.dt.int32
AF = mybir.ActivationFunctionType
AX = mybir.AxisListType
OP = mybir.AluOpType

NCORES = 8
T = 2048
D = 1024
HID = 3072
NSEQ = 2
TOK = NSEQ * T
CPS = T // 128
EPS = 1e-5
QS = 1.0 / 127.0
KB = 1024


class Marker:
    __slots__ = ("sem", "val")

    def __init__(self, sem, val):
        self.sem = sem
        self.val = val


class Prog:

    ENGINES = ("sync", "gpsimd", "vector", "scalar", "tensor")

    KPOOL = {"s": 14, "g": 14, "a": 48}

    def __init__(self, nc):
        self.nc = nc
        self.ops = {e: [] for e in self.ENGINES}
        self.sems = {}
        for name in ("te", "dv", "ac", "sy", "gp"):
            h = nc.semaphore(name).__enter__()
            self.sems[name] = [h, 0]
        for q in ("s", "g", "a"):
            for j in range(self.KPOOL[q]):
                name = f"q{q}{j}"
                h = nc.semaphore(name).__enter__()
                self.sems[name] = [h, 0]
        self.qprefix = {"sync": "qs", "gpsimd": "qg", "scalar": "qa"}
        self.qcount = {"sync": 0, "gpsimd": 0, "scalar": 0}
        self.csem = {"vector": "dv", "scalar": "ac", "tensor": "te",
                     "sync": "sy", "gpsimd": "gp"}
        self.pending_fence = {e: [] for e in self.ENGINES}

    def cur(self, name):
        return Marker(name, self.sems[name][1])

    def queue_tail(self, queue):
        pre = self.qprefix[queue]
        return [self.cur(f"{pre}{j}") for j in range(self.KPOOL[pre[1]])
                if self.sems[f"{pre}{j}"][1] > 0]

    def fence(self, ms=None):
        if ms is None:
            ms = self.fence_markers()
        for e in self.ENGINES:
            self.pending_fence[e] = ms

    def fence_markers(self):
        return ([self.cur(n) for n in ("te", "dv", "ac")]
                + self.queue_tail("sync") + self.queue_tail("gpsimd")
                + self.queue_tail("scalar"))

    def _take_fence(self, eng):
        f = self.pending_fence[eng]
        self.pending_fence[eng] = []
        return f

    def dma(self, queue, out, in_, waits=(), **kw):
        i = self.qcount[queue]
        self.qcount[queue] = i + 1
        name = f"{self.qprefix[queue]}{i % self.KPOOL[self.qprefix[queue][1]]}"
        prev = self.sems[name][1]
        self.sems[name][1] = prev + 16
        val = prev + 16
        h = self.sems[name][0]
        waits = list(waits)
        if prev > 0:
            waits.append(Marker(name, prev))

        def fn(e, out=out, in_=in_, kw=kw, h=h):
            return e.dma_start(out=out, in_=in_, **kw).then_inc(h, 16)

        eng = "scalar" if queue == "scalar" else queue
        self.ops[eng].append((fn, self._resolve(waits, eng), None))
        return Marker(name, val)

    AUTOCHAIN = ("vector", "scalar")

    def op(self, eng, fn, waits=(), inc=True):
        m = None
        incpair = None
        name = self.csem[eng]
        waits = list(waits)
        if eng in self.AUTOCHAIN and self.sems[name][1] > 0:
            waits.append(Marker(name, self.sems[name][1]))
        if inc:
            self.sems[name][1] += 1
            m = Marker(name, self.sems[name][1])
            incpair = (self.sems[name][0], 1)
        self.ops[eng].append((fn, self._resolve(waits, eng), incpair))
        return m

    def _resolve(self, waits, eng):
        out = []
        for w in list(waits) + self._take_fence(eng):
            if w is None:
                continue
            if isinstance(w, (list, tuple)):
                out.extend((m.sem, m.val) for m in w if m is not None)
            else:
                out.append((w.sem, w.val))
        return out

    def emit(self, block):
        def run(e, eng_name):
            own = self.csem.get(eng_name)
            skip_own = eng_name not in self.AUTOCHAIN
            ew = {}
            for fn, waits, incpair in self.ops[eng_name]:
                for sem_name, val in waits:
                    if sem_name == own and skip_own:
                        continue
                    if ew.get(sem_name, 0) >= val:
                        continue
                    e.wait_ge(self.sems[sem_name][0], val)
                    ew[sem_name] = val
                inst = fn(e)
                if incpair is not None:
                    inst.then_inc(*incpair)

        block.sync(lambda e: run(e, "sync"))
        block.gpsimd(lambda e: run(e, "gpsimd"))
        block.vector(lambda e: run(e, "vector"))
        block.scalar(lambda e: run(e, "scalar"))
        block.tensor(lambda e: run(e, "tensor"))


class Res:
    def __init__(self):
        self.ready = []
        self.free = []


def build(debug=False, upto='all', nchunks=16, sub='full'):
    nc = bass.Bass("TRN2", target_bir_lowering=False)

    x_in = nc.dram_tensor("x", [TOK, D], F32, kind="ExternalInput").ap()
    ln_g = nc.dram_tensor("ln_g", [D], F32, kind="ExternalInput").ap()
    ln_b = nc.dram_tensor("ln_b", [D], F32, kind="ExternalInput").ap()
    Wq = nc.dram_tensor("Wq", [D, D], F32, kind="ExternalInput").ap()
    Wk = nc.dram_tensor("Wk", [D, D], F32, kind="ExternalInput").ap()
    Wv = nc.dram_tensor("Wv", [D, D], F32, kind="ExternalInput").ap()
    bq = nc.dram_tensor("bq", [D], F32, kind="ExternalInput").ap()
    bk = nc.dram_tensor("bk", [D], F32, kind="ExternalInput").ap()
    bv = nc.dram_tensor("bv", [D], F32, kind="ExternalInput").ap()
    wg_i = nc.dram_tensor("wg", [HID, D], I32, kind="ExternalInput").ap()
    wu_i = nc.dram_tensor("wu", [HID, D], I32, kind="ExternalInput").ap()
    wd_i = nc.dram_tensor("wd", [D, HID], I32, kind="ExternalInput").ap()
    ident_in = nc.dram_tensor("ident", [128, 128], F32, kind="ExternalInput").ap()
    out_hbm = nc.dram_tensor("out", [TOK, D], F32, kind="ExternalOutput").ap()

    ik = "ExternalOutput" if debug else "Internal"
    s_wg = nc.dram_tensor("s_wg", [HID, D], BF16, kind="Internal").ap()
    s_wu = nc.dram_tensor("s_wu", [HID, D], BF16, kind="Internal").ap()
    s_wd = nc.dram_tensor("s_wd", [D, HID], BF16, kind="Internal").ap()
    s_kT = nc.dram_tensor("s_kT", [NSEQ, 8, 128, T], F32, kind=ik).ap()
    s_v = nc.dram_tensor("s_v", [NSEQ, T, D], F32, kind=ik).ap()
    s_qT = nc.dram_tensor("s_qT", [NSEQ, 8, 128, T], F32, kind=ik).ap()
    s_r = nc.dram_tensor("s_r", [NSEQ, T, D], F32, kind=ik).ap()
    s_h1 = nc.dram_tensor("s_h1", [TOK, D], F32, kind=ik).ap()
    s_hnT_hi = nc.dram_tensor("s_hnT_hi", [8, 128, TOK], BF16, kind="Internal").ap()
    s_hnT_lo = nc.dram_tensor("s_hnT_lo", [8, 128, TOK], BF16, kind="Internal").ap()
    s_brow_k = nc.dram_tensor("s_brow_k", [D], F32, kind="Internal").ap()

    arena = nc.alloc_sbuf_tensor("arena", [128, 192 * KB], mybir.dt.uint8)
    _base = nc.lookup_mloc(arena).addr

    def A(name, shape, dtype, offset):
        return nc.alloc_sbuf_tensor_at(name, shape, dtype, offset=_base + offset)
    g_col = A("g_col", [128, 8], F32, offset=0)
    bcol_hi = A("bcol_hi", [128, 8], BF16, offset=64)
    bcol_lo = A("bcol_lo", [128, 8], BF16, offset=96)
    bk_col = A("bk_col", [128, 8], F32, offset=128)
    Mk = A("Mk", [128, 8], F32, offset=192)
    rk = A("rk", [128, 8], F32, offset=224)
    nMk = A("nMk", [128, 8], F32, offset=256)
    bcol_f = A("bcol_f", [128, 8], F32, offset=288)
    rq_all = A("rq_all", [128, CPS, 16], F32, offset=320)
    stat = A("stat", [128, 48], F32, offset=1344)
    ones_t = A("ones_t", [128, 128], F32, offset=1536)
    ident = A("ident", [128, 128], F32, offset=2048)
    ones_bf = A("ones_bf", [1, 128], BF16, offset=2560)
    bq_col = A("bq_col", [128, 8], F32, offset=2816)
    ones2 = A("ones2", [128, 2], F32, offset=2880)
    brow_v_hi = A("brow_v_hi", [1, D], BF16, offset=2816 + 4 * KB)
    brow_v_lo = A("brow_v_lo", [1, D], BF16, offset=2816 + 6 * KB)
    attn_all = A("attn_all", [128, 8, 128], F32, offset=2816 + 8 * KB)

    WOFF = 16 * KB
    wT = {}
    for i, (w, ver) in enumerate([(w, v) for w in "qkv" for v in ("hi", "lo")]):
        wT[(w, ver)] = A(f"w{w}T_{ver}", [128, 8, D], BF16, offset=WOFF + i * 16 * KB)

    P1 = 112 * KB
    x0_r = A("x0_r", [128, 2, D], F32, offset=P1)
    n1_t = A("n1_t", [128, D], F32, offset=P1 + 8 * KB)
    xn_r = A("xn_r", [128, 2, D], F32, offset=P1 + 12 * KB)
    r_r = A("r_r", [128, 2, D], F32, offset=P1 + 20 * KB)
    qT_stg = A("qT_stg", [128, 2, 512], F32, offset=P1 + 44 * KB)
    kT_stg = A("kT_stg", [128, 2, 512], F32, offset=P1 + 52 * KB)
    v_stg = A("v_stg", [128, 2, D], F32, offset=P1 + 56 * KB)
    xnT_hi = A("xnT_hi", [128, 8, 4, 128], BF16, offset=P1 + 64 * KB)
    xnT_lo = A("xnT_lo", [128, 8, 4, 128], BF16, offset=P1 + 72 * KB)

    kTe2 = A("kTe2", [128, 2, T], F32, offset=P1)
    ekTq = A("ekTq", [128, 2, 512], F32, offset=P1 + 16 * KB)
    ek_r = A("ek_r", [128, 2, 4, 128], F32, offset=P1 + 20 * KB)
    v_p2 = A("v_p2", [128, 2, 16, 128], F32, offset=P1 + 24 * KB)

    qT_c = A("qT_c", [128, 2, 8, 128], F32, offset=P1 + 40 * KB)
    r_c = A("r_c", [128, 2, D], F32, offset=P1 + 48 * KB)
    h1_st = A("h1_st", [128, 2, D], F32, offset=P1 + 56 * KB)
    h1n_t = A("h1n_t", [128, 2, D], F32, offset=P1 + 64 * KB)
    hnT_hi_st = A("hnT_hi_st", [128, 8, 256], BF16, offset=P1 + 72 * KB)
    hnT_lo_st = A("hnT_lo_st", [128, 8, 256], BF16, offset=P1 + 76 * KB)

    pw_f = A("pw_f", [128, 4, D], F32, offset=P1 + 28 * KB)
    ptmp = A("ptmp", [128, 2, D], F32, offset=P1 + 44 * KB)
    brow = {w: A(f"brow_{w}", [1, D], F32, offset=P1 + 52 * KB + i * 4 * KB)
            for i, w in enumerate("qkv")}

    wgT = A("wgT", [128, 8, HID], BF16, offset=16 * KB)
    wuT = A("wuT", [128, 8, HID], BF16, offset=64 * KB)
    hnT_hi_b = A("hnT_hi_b", [128, 2, 8, 256], BF16, offset=112 * KB)
    hnT_lo_b = A("hnT_lo_b", [128, 2, 8, 256], BF16, offset=120 * KB)
    wdT_t = A("wdT_t", [128, 24, D], BF16, offset=128 * KB)
    g1_t = A("g1_t", [128, 2, 256], F32, offset=176 * KB)
    hmT_hi = A("hmT_hi", [128, 4, 256], BF16, offset=178 * KB)
    hmT_lo = A("hmT_lo", [128, 4, 256], BF16, offset=180 * KB)
    out_stg = A("out_stg", [128, 2, D], F32, offset=182 * KB)
    h1_b = A("h1_b", [128, 4, D], F32, offset=0)

    def wdT(hc):
        return wdT_t.ap()[:, hc, :]

    pb = [nc.alloc_psum_tensor(f"pb{i}", [128, 512], F32) for i in range(8)]

    P = Prog(nc)
    bank = [Res() for _ in range(8)]

    m_ident = P.dma("sync", ident[:], ident_in[:, :])
    m_g_row = P.dma("sync", pw_f[0:1, 0, :], ln_g[:])
    m_b_row = P.dma("sync", pw_f[0:1, 1, :], ln_b[:])
    m_bvec = {}
    for w, src in (("q", bq), ("k", bk), ("v", bv)):
        m_bvec[w] = P.dma("sync", brow[w][0:1, :], src[:])
    pre_x0 = {}
    for c0 in range(2):
        pre_x0[c0] = P.dma("sync", x0_r[:, c0, :],
                           x_in[c0 * 128:(c0 + 1) * 128, :])
    m_ones = P.op("vector", lambda e: e.memset(ones_t[0:1, :], 1.0))
    m_onesb = P.op("vector", lambda e: e.memset(ones_bf[0:1, :], 1.0))

    def colize(dst, src_row, row_marker, bnk):
        mms = []
        for cc in range(8):
            rw = row_marker if isinstance(row_marker, list) else [row_marker]
            mms.append(P.op("tensor", lambda e, cc=cc, bnk=bnk, src_row=src_row:
                            e.matmul(pb[bnk][:, cc:cc + 1],
                                     src_row[0:1, cc * 128:(cc + 1) * 128],
                                     ones_t[0:1, 0:1], start=True, stop=True),
                            waits=rw + [m_ones] + (bank[bnk].free
                                                   if cc == 0 else []),
                            inc=(cc == 7)))
        mc = P.op("vector", lambda e, dst=dst, bnk=bnk: e.tensor_copy(
            out=dst, in_=pb[bnk][:, 0:8]), waits=[mms[-1]])
        bank[bnk].free = [mc]
        return mc

    m_gcol = colize(g_col[:], pw_f.ap()[:, 0, :], m_g_row, 6)
    m_bcolf = colize(bcol_f[:], pw_f.ap()[:, 1, :], m_b_row, 7)

    m_cast = {"g": [], "u": [], "d": []}
    cast_slices = []
    for j in range(8):
        cast_slices.append(("g", s_wg[j * 384:(j + 1) * 384, :],
                            wg_i[j * 384:(j + 1) * 384, :]))
    for j in range(8):
        cast_slices.append(("u", s_wu[j * 384:(j + 1) * 384, :],
                            wu_i[j * 384:(j + 1) * 384, :]))
    for j in range(8):
        cast_slices.append(("d", s_wd[j * 128:(j + 1) * 128, :],
                            wd_i[j * 128:(j + 1) * 128, :]))

    def pump_cast():
        if cast_slices:
            w, dst, srcq = cast_slices.pop(0)
            m_cast[w].append(P.dma("gpsimd", dst, srcq))

    def emit_casts():
        while cast_slices:
            pump_cast()

    pw_free = [[m_gcol], [m_bcolf], [], []]
    ptmp_free = [[], [], [], []]
    m_wT = {}
    for wi, (w, wsrc) in enumerate((("v", Wv), ("q", Wq), ("k", Wk))):
        for rb in range(8):
            sl = (wi * 8 + rb) % 4
            rows = slice(rb * 128, (rb + 1) * 128)
            mf = P.dma("sync", pw_f[:, sl, :],
                       wsrc[rows, :], waits=pw_free[sl])
            trs = []
            for half in range(2):
                bnk = 6 + half
                mlast = None
                for j in range(4):
                    d = half * 4 + j
                    mlast = P.op("tensor",
                                 lambda e, bnk=bnk, j=j, d=d, sl=sl:
                                 e.transpose(
                                     pb[bnk][:, j * 128:(j + 1) * 128],
                                     pw_f[:, sl, d * 128:(d + 1) * 128],
                                     ident[:]),
                                 waits=[mf, m_ident] + bank[bnk].free,
                                 inc=(j == 3))
                trs.append(mlast)
            pw_free[sl] = trs[:]
            psl2 = sl % 2
            mtc0 = P.op("scalar", lambda e, psl2=psl2: e.activation(
                out=ptmp[:, psl2, 0:512], in_=pb[6][:], func=AF.Copy),
                waits=[trs[0]] + ptmp_free[psl2])
            mtc1 = P.op("vector", lambda e, psl2=psl2: e.tensor_copy(
                out=ptmp[:, psl2, 512:1024], in_=pb[7][:]),
                waits=[trs[1]] + ptmp_free[psl2])
            bank[6].free = [mtc0]
            bank[7].free = [mtc1]
            mhiw = P.op("scalar", lambda e, psl2=psl2, w=w, rb=rb: e.activation(
                out=wT[(w, "hi")].ap()[:, :, rb * 128:(rb + 1) * 128],
                in_=ptmp.ap()[:, psl2, :].rearrange("p (a b) -> p a b", b=128),
                func=AF.Copy), waits=[mtc0, mtc1])
            mlow = P.op("vector", lambda e, psl2=psl2, w=w, rb=rb: e.tensor_sub(
                out=wT[(w, "lo")].ap()[:, :, rb * 128:(rb + 1) * 128],
                in0=ptmp.ap()[:, psl2, :].rearrange("p (a b) -> p a b", b=128),
                in1=wT[(w, "hi")].ap()[:, :, rb * 128:(rb + 1) * 128]),
                waits=[mhiw])
            ptmp_free[psl2] = [mlow]
            m_wT[(w, rb)] = [mhiw, mlow]

    m_bch = P.op("scalar", lambda e: e.activation(
        out=bcol_hi[:], in_=bcol_f[:], func=AF.Copy), waits=[m_bcolf])
    m_bcl = P.op("vector", lambda e: e.tensor_sub(
        out=bcol_lo[:], in0=bcol_f[:], in1=bcol_hi[:]), waits=[m_bcolf, m_bch])
    m_az = P.op("vector", lambda e: e.memset(attn_all[:], 0.0))

    m_brow = {}
    for w in "qkv":
        wdeps = [m for rb in range(8) for m in m_wT[(w, rb)]]
        lasts = []
        for half in range(2):
            hs = slice(half * 512, (half + 1) * 512)
            n = 0
            last = None
            for d in range(8):
                for (bc, wv) in ((bcol_hi, "hi"), (bcol_lo, "hi"), (bcol_hi, "lo")):
                    n += 1
                    last = P.op("tensor", lambda e, d=d, bc=bc, wv=wv, hs=hs, w=w,
                                half=half, st=(n == 1), sp=(n == 24): e.matmul(
                        pb[half][0:1, :], bc[:, d:d + 1], wT[(w, wv)][:, d, hs],
                        start=st, stop=sp),
                        waits=([m_bch, m_bcl] + wdeps + bank[half].free)
                        if n == 1 else (),
                        inc=(n == 24))
            lasts.append(last)
        m1 = P.op("vector", lambda e, w=w: e.tensor_add(
            out=brow[w][0:1, 0:512], in0=brow[w][0:1, 0:512], in1=pb[0][0:1, :]),
            waits=[lasts[0], m_bvec[w]])
        m2 = P.op("vector", lambda e, w=w: e.tensor_add(
            out=brow[w][0:1, 512:1024], in0=brow[w][0:1, 512:1024],
            in1=pb[1][0:1, :]), waits=[lasts[1]])
        bank[0].free = [m1]
        bank[1].free = [m2]
        m_brow[w] = [m1, m2]

    m_btile = {}
    for w, (rhi, rlo) in (("v", (brow_v_hi, brow_v_lo)),):
        mh_ = P.op("scalar", lambda e, w=w, rhi=rhi: e.activation(
            out=rhi[0:1, :], in_=brow[w][0:1, :], func=AF.Copy),
            waits=m_brow[w])
        ml_ = P.op("vector", lambda e, w=w, rhi=rhi, rlo=rlo: e.tensor_sub(
            out=rlo[0:1, :], in0=brow[w][0:1, :], in1=rhi[0:1, :]),
            waits=m_brow[w] + [mh_])
        m_btile[w] = [mh_, ml_]
    mz2 = P.op("vector", lambda e: e.memset(ones2[:], 0.0))
    mo2a = P.op("vector", lambda e: e.memset(ones2[0:64, 0:1], 1.0), waits=[mz2])
    m_ones2 = P.op("vector", lambda e: e.memset(ones2[64:128, 1:2], 1.0),
                   waits=[mo2a])

    m_bkc = colize(bk_col[:], brow["k"], m_brow["k"], 6)
    m_bqc = colize(bq_col[:], brow["q"], m_brow["q"], 7)


    v_ready = ([m for rb in range(8) for m in m_wT[("v", rb)]]
               + m_btile["v"] + [m_gcol, m_ident, m_onesb])
    qk_ready = ([m for w in "qk" for rb in range(8) for m in m_wT[(w, rb)]]
                + [m_bkc, m_bqc, m_ones2])

    x0_res = [Res(), Res()]
    xn_res = [Res(), Res()]
    r_res = [Res(), Res()]
    q_res = [Res(), Res()]
    eq_res = [Res(), Res()]
    qTs_res = [Res(), Res()]
    kTs_res = [Res(), Res()]
    vs_res = [Res(), Res()]
    xnT_res = [Res() for _ in range(4)]
    n1_free = []
    m_ph1end = {}
    m_negMk = {}
    m_kstore = {}
    m_vstore = {}
    m_qTstore = {}
    m_rstore = {}
    m_h1store = {}
    m_hnT_store = []
    m_attn = {}

    def ln_stats(src_ap, dep_markers, mean_slot, rstd_slot):
        bnv = stat.ap()[:, 32:44].rearrange("p (g x) -> p g x", x=6)
        mb1 = P.op("vector", lambda e: e.bn_stats(
            out=stat.ap()[:, 32:38], in_=src_ap[:, 0:512]), waits=dep_markers)
        mb2 = P.op("vector", lambda e: e.bn_stats(
            out=stat.ap()[:, 38:44], in_=src_ap[:, 512:1024]), waits=dep_markers)
        mag = P.op("vector", lambda e, ms=mean_slot: e.bn_aggr(
            out=stat[:, ms:ms + 2], in_=bnv), waits=[mb1, mb2])
        mvp = P.op("vector", lambda e, ms=mean_slot: e.tensor_scalar_add(
            out=stat[:, ms + 2:ms + 3], in0=stat[:, ms + 1:ms + 2],
            scalar1=EPS), waits=[mag])
        msr = P.op("scalar", lambda e, ms=mean_slot: e.activation(
            out=stat[:, ms + 3:ms + 4], in_=stat[:, ms + 2:ms + 3],
            func=AF.Sqrt), waits=[mvp])
        mrc = P.op("vector", lambda e, ms=mean_slot, rs=rstd_slot: e.reciprocal(
            out=stat[:, rs:rs + 1], in_=stat[:, ms + 3:ms + 4]), waits=[msr])
        return mrc

    def phase1(s):
        nonlocal n1_free
        m_mkinit = P.op("vector", lambda e: e.memset(Mk[:], -3.0e38))

        def S1(c):
            nonlocal n1_free
            tok0 = s * T + c * 128
            sl = c % 2
            x0 = x0_r[:, sl, :]
            if s == 0 and c < 2:
                mx = pre_x0[c]
            else:
                mx = P.dma("sync", x0, x_in[tok0:tok0 + 128, :],
                           waits=x0_res[sl].free)
            mrc1 = ln_stats(x0, [mx] + n1_free, 0, 4)
            mnb1 = P.op("vector", lambda e: e.tensor_scalar(
                out=stat[:, 10:11], in0=stat[:, 0:1], scalar1=stat[:, 4:5],
                scalar2=-1.0, op0=OP.mult, op1=OP.mult), waits=[mrc1])
            mn1 = P.op("scalar", lambda e, x0=x0: e.activation(
                out=n1_t[:], in_=x0, func=AF.Identity,
                bias=stat[:, 10:11], scale=stat[:, 4:5]),
                waits=[mnb1] + n1_free)
            mr = P.op("vector", lambda e, x0=x0, sl=sl: e.tensor_add(
                out=r_r[:, sl, :], in0=x0, in1=n1_t[:]),
                waits=[mn1] + r_res[sl].free)
            x0_res[sl].free = [mr]
            mrst = P.dma("gpsimd", s_r[s, c * 128:(c + 1) * 128, :],
                         r_r[:, sl, :], waits=[mr])
            r_res[sl].free = [mrst]
            m_rstore[(s, c)] = mrst
            mrc2 = ln_stats(n1_t.ap(), [mn1], 5, 9)
            mnb2 = P.op("vector", lambda e: e.tensor_scalar(
                out=stat[:, 11:12], in0=stat[:, 5:6], scalar1=stat[:, 9:10],
                scalar2=-1.0, op0=OP.mult, op1=OP.mult), waits=[mrc2])
            mxn = P.op("scalar", lambda e, sl=sl: e.activation(
                out=xn_r[:, sl, :], in_=n1_t[:], func=AF.Identity,
                bias=stat[:, 11:12], scale=stat[:, 9:10]),
                waits=[mnb2] + xn_res[sl].free)
            xn_res[sl].ready = [mxn]
            return {"mxn": mxn}

        def S2(c, st1):
            nonlocal n1_free
            sl = c % 2
            tsl = c % 4
            mxn = st1["mxn"]
            tms = []
            for half in range(2):
                bnk = half
                mlast = None
                for j in range(4):
                    d = half * 4 + j
                    mlast = P.op("tensor",
                                 lambda e, bnk=bnk, j=j, d=d, sl=sl:
                                 e.transpose(
                                     pb[bnk][:, j * 128:(j + 1) * 128],
                                     xn_r[:, sl, d * 128:(d + 1) * 128],
                                     ident[:]),
                                 waits=[mxn, m_ident] + bank[bnk].free
                                 + xnT_res[tsl].free,
                                 inc=(j == 3))
                tms.append(mlast)
            xn_res[sl].free = tms[:]
            evac0 = []
            for j in range(4):
                d = j
                evac0.append(P.op("scalar", lambda e, j=j, d=d: e.activation(
                    out=n1_t[:, d * 128:(d + 1) * 128],
                    in_=pb[0][:, j * 128:(j + 1) * 128],
                    func=AF.Copy, scale=g_col[:, d:d + 1]),
                    waits=[tms[0], m_gcol]))
            evac1 = []
            for j in range(4):
                d = 4 + j
                evac1.append(P.op("vector",
                                  lambda e, j=j, d=d: e.tensor_scalar_mul(
                                      out=n1_t[:, d * 128:(d + 1) * 128],
                                      in0=pb[1][:, j * 128:(j + 1) * 128],
                                      scalar1=g_col[:, d:d + 1]),
                                  waits=[tms[1]]))
            mhi = P.op("scalar", lambda e, tsl=tsl: e.activation(
                out=xnT_hi.ap()[:, :, tsl, :], in_=n1_t[:], func=AF.Copy),
                waits=evac0 + evac1)
            mlo = P.op("vector", lambda e, tsl=tsl: e.tensor_sub(
                out=xnT_lo.ap()[:, :, tsl, :], in0=n1_t[:],
                in1=xnT_hi.ap()[:, :, tsl, :]), waits=[mhi])
            evs = [mhi, mlo]
            bank[0].free = [mhi] + evac0
            bank[1].free = [mhi, mlo] + evac1
            xnT_res[tsl].ready = evs
            n1_free = [mlo]
            return {"evs": evs}

        def S3(c, st2):
            tsl = c % 4
            n = 0
            mhi_m, mlo_m = st2["evs"]
            w0 = (v_ready + [mhi_m] + bank[4].free + bank[5].free)
            for (xt, wver, movs) in (
                (xnT_hi, "hi", (("v", "hi"), ("v", "lo"))),
                (xnT_lo, "lo", (("v", "hi"),)),
            ):
                for d in range(8):
                    for (w, mv) in movs:
                        for half in range(2):
                            n += 1
                            bnk = 4 + half
                            st = (d == 0 and wver == "hi" and mv == "hi")
                            P.op("tensor",
                                 lambda e, bnk=bnk, d=d, xt=xt, tsl=tsl,
                                 mv=mv, half=half, st=st: e.matmul(
                                     pb[bnk][:], xt[:, d, tsl, :],
                                     wT[("v", mv)][:, d,
                                                   half * 512:(half + 1) * 512],
                                     start=st, stop=False),
                                 waits=w0 if n == 1
                                 else ([mlo_m] if (wver == "lo" and d == 0
                                                   and half == 0) else ()),
                                 inc=False)
            mqv = None
            for half in range(2):
                hs = slice(half * 512, (half + 1) * 512)
                bnk = 4 + half
                P.op("tensor", lambda e, bnk=bnk, hs=hs:
                     e.matmul(pb[bnk][:], ones_bf[0:1, 0:128],
                              brow_v_hi[0:1, hs], start=False, stop=False),
                     inc=False)
                mqv = P.op("tensor", lambda e, bnk=bnk, hs=hs:
                           e.matmul(pb[bnk][:], ones_bf[0:1, 0:128],
                                    brow_v_lo[0:1, hs], start=False,
                                    stop=True),
                           inc=(half == 1))
            return {"mqv": mqv}

        def S4(c, st3):
            mqv = st3["mqv"]
            qsl = c % 2
            mve1 = P.op("scalar", lambda e, qsl=qsl: e.activation(
                out=v_stg[:, qsl, 0:512], in_=pb[4][:], func=AF.Copy),
                waits=[mqv] + vs_res[qsl].free)
            mve2 = P.op("vector", lambda e, qsl=qsl: e.tensor_copy(
                out=v_stg[:, qsl, 512:1024], in_=pb[5][:]),
                waits=[mqv] + vs_res[qsl].free)
            bank[4].free = [mve1]
            bank[5].free = [mve2]
            mvs = P.dma("gpsimd", s_v[s, c * 128:(c + 1) * 128, :],
                        v_stg[:, qsl, :], waits=[mve1, mve2])
            vs_res[qsl].free = [mvs]
            m_vstore[(s, c)] = mvs

        def kblock(blk):
            ktok = slice(blk * 512, (blk + 1) * 512)
            kdeps = [m for t4 in range(4) for m in xnT_res[t4].ready] \
                + (qk_ready if blk == 0 else [])
            klast = None
            for e_ in range(8):
                for (wname, bbase, res, stg, stores, mcol) in (
                    ("q", 2, qTs_res, qT_stg, m_qTstore, m_bqc),
                    ("k", 6, kTs_res, kT_stg, m_kstore, m_bkc),
                ):
                    bnk = bbase + (e_ % 2)
                    n = 0
                    last = None
                    for d in range(8):
                        for (wver, xt) in (("hi", xnT_hi), ("hi", xnT_lo),
                                           ("lo", xnT_hi)):
                            n += 1
                            st = (n == 1)
                            sp = (n == 24)
                            last = P.op("tensor",
                                        lambda e, bnk=bnk, d=d, e_=e_, xt=xt,
                                        wver=wver, st=st, sp=sp, wname=wname:
                                        e.matmul(
                                            pb[bnk][:],
                                            wT[(wname, wver)][
                                                :, d, e_ * 128:(e_ + 1) * 128],
                                            xt.ap()[:, d, :, :].rearrange(
                                                "p a b -> p (a b)"),
                                            start=st, stop=sp),
                                        waits=(kdeps + bank[bnk].free)
                                        if n == 1 else (),
                                        inc=(n == 24))
                    klast = last
                    ksl = e_ % 2
                    if wname == "q":
                        mke = P.op("scalar",
                                   lambda e, bnk=bnk, ksl=ksl, e_=e_:
                                   e.activation(
                                       out=qT_stg[:, ksl, :], in_=pb[bnk][:],
                                       func=AF.Exp,
                                       bias=bq_col[:, e_:e_ + 1]),
                                   waits=[last, m_bqc] + qTs_res[ksl].free)
                        bank[bnk].free = [mke]
                        mks = P.dma("gpsimd", s_qT[s, e_, :, ktok],
                                    qT_stg[:, ksl, :], waits=[mke])
                        qTs_res[ksl].free = [mks]
                        m_qTstore[(s, e_, blk)] = mks
                    else:
                        mke = P.op("scalar",
                                   lambda e, bnk=bnk, ksl=ksl, e_=e_:
                                   e.activation(
                                       out=kT_stg[:, ksl, :], in_=pb[bnk][:],
                                       func=AF.Identity,
                                       bias=bk_col[:, e_:e_ + 1]),
                                   waits=[last, m_bkc] + kTs_res[ksl].free)
                        bank[bnk].free = [mke]
                        mkm = P.op("vector", lambda e, ksl=ksl: e.tensor_reduce(
                            out=stat[:, 14:15], in_=kT_stg[:, ksl, :],
                            axis=AX.X, op=OP.max), waits=[mke])
                        P.op("vector", lambda e, e_=e_: e.tensor_max(
                            out=Mk[:, e_:e_ + 1], in0=Mk[:, e_:e_ + 1],
                            in1=stat[:, 14:15]), waits=[mkm, m_mkinit])
                        mks = P.dma("gpsimd", s_kT[s, e_, :, ktok],
                                    kT_stg[:, ksl, :], waits=[mke])
                        kTs_res[ksl].free = [mks, mkm]
                        m_kstore[(s, e_, blk)] = mks
            for t4 in range(4):
                xnT_res[t4].free = [klast]

        nch = min(nchunks, CPS)
        st1 = {0: S1(0)} if nch > 0 else {}
        st2 = {}
        st3 = {}
        for c in range(nch):
            st2[c] = S2(c, st1[c])
            if c - 1 >= 0:
                S4(c - 1, st3[c - 1])
            if c + 1 < nch:
                st1[c + 1] = S1(c + 1)
            st3[c] = S3(c, st2[c])
            if c % 4 == 3 and (c + 1) * 128 <= T:
                kblock(c // 4)
            if s == 1:
                pump_cast()
        for c in range(max(nch - 1, 0), nch):
            S4(c, st3[c])
        if nch >= CPS:
            m_negMk[s] = P.op("vector", lambda e: e.tensor_scalar_mul(
                out=nMk[:], in0=Mk[:], scalar1=-1.0))
        m_ph1end[s] = P.cur("te")

    ph2_state = {}

    def phase2_prefetch(s):
        ek_res = [Res() for _ in range(2)]
        kTe_res = [Res(), Res()]
        ekq_res = [Res(), Res()]
        vp_res = [Res(), Res()]
        loads = {}

        def load_pair(p_):
            psl = p_ % 2
            mload = P.dma("sync", kTe2[:, psl, :], s_kT[s, p_, :, :],
                          waits=[m_kstore[(s, p_, blk)] for blk in range(4)]
                          + kTe_res[psl].free)
            mvp = P.dma("sync", v_p2[:, psl, :, :],
                        s_v[s].rearrange("(c p) j -> p c j", p=128)[
                            :, :, p_ * 128:(p_ + 1) * 128],
                        waits=[m_vstore[(s, c)] for c in range(CPS)]
                        + [m_rstore[(s, c)] for c in (CPS - 2, CPS - 1)]
                        + vp_res[psl].free)
            loads[p_] = (mload, mvp)

        ph2_state[s] = (ek_res, kTe_res, ekq_res, vp_res, loads, load_pair)
        load_pair(0)
        load_pair(1)

    def phase2(s):
        (ek_res, kTe_res, ekq_res, vp_res, loads, load_pair) = ph2_state[s]
        mneg = m_negMk[s]
        for p_ in range(8):
            psl = p_ % 2
            mload, mvp = loads[p_]
            abnk = 2 + (p_ % 2)
            last_mm = None
            exp_marks = []
            msk_last = None
            trs_q = {}

            def emit_exp_tr(qt):
                nonlocal msk_last
                esl2 = qt % 2
                mexp = P.op("scalar", lambda e, p_=p_, psl=psl, qt=qt,
                            esl2=esl2: e.activation(
                                out=ekTq[:, esl2, :],
                                in_=kTe2[:, psl, qt * 512:(qt + 1) * 512],
                                func=AF.Exp, bias=nMk[:, p_:p_ + 1]),
                            waits=[mload, mneg] + ekq_res[esl2].free)
                exp_marks.append(mexp)
                if qt == 0:
                    msk_last = P.op("vector",
                                    lambda e, esl2=esl2: e.tensor_reduce(
                                        out=stat[:, 15:16],
                                        in_=ekTq[:, esl2, :], axis=AX.X,
                                        op=OP.add), waits=[mexp])
                else:
                    mr1 = P.op("vector", lambda e, esl2=esl2: e.tensor_reduce(
                        out=stat[:, 13:14], in_=ekTq[:, esl2, :], axis=AX.X,
                        op=OP.add), waits=[mexp])
                    msk_last = P.op("vector", lambda e: e.tensor_add(
                        out=stat[:, 15:16], in0=stat[:, 15:16],
                        in1=stat[:, 13:14]), waits=[mr1])
                btr = qt % 2
                last_tr = None
                for t4 in range(4):
                    last_tr = P.op("tensor", lambda e, t4=t4, btr=btr,
                                   esl2=esl2: e.transpose(
                                       pb[btr][:, t4 * 128:(t4 + 1) * 128],
                                       ekTq[:, esl2, t4 * 128:(t4 + 1) * 128],
                                       ident[:]),
                                   waits=([mexp, m_ident] + bank[btr].free
                                          + ek_res[qt % 2].free)
                                   if t4 == 0 else [mexp],
                                   inc=(t4 == 3))
                trs_q[qt] = last_tr
                ekq_res[esl2].free = [last_tr, msk_last]

            emit_exp_tr(0)
            for qt in range(4):
                if qt + 1 < 4:
                    emit_exp_tr(qt + 1)
                btr = qt % 2
                mev = P.op("vector", lambda e, btr=btr, qt=qt: e.tensor_copy(
                    out=ek_r.ap()[:, qt % 2, :, :].rearrange(
                        "p a b -> p (a b)"),
                    in_=pb[btr][:]), waits=[trs_q[qt]] + ek_res[qt % 2].free)
                bank[btr].free = [mev]
                for t4 in range(4):
                    t = qt * 4 + t4
                    mmm = P.op("tensor", lambda e, t=t, t4=t4, qt=qt, psl=psl,
                               abnk=abnk: e.matmul(
                                   pb[abnk][:, 0:128], ek_r[:, qt % 2, t4, :],
                                   v_p2[:, psl, t, :],
                                   start=(t == 0), stop=(t == 15)),
                               waits=[mev, mvp]
                               + (bank[abnk].free if t == 0 else []),
                               inc=(t4 == 3))
                    last_mm = mmm
                ek_res[qt % 2].free = [last_mm]
            kTe_res[psl].free = [exp_marks[-1]]
            vp_res[psl].free = [last_mm]
            if p_ + 2 < 8:
                load_pair(p_ + 2)
            pump_cast()
            mrk = P.op("vector", lambda e, p_=p_: e.reciprocal(
                out=rk[:, p_:p_ + 1], in_=stat[:, 15:16]), waits=[msk_last])
            ma1 = P.op("vector", lambda e, p_=p_, abnk=abnk: e.tensor_scalar_mul(
                out=attn_all[0:64, p_, 0:64], in0=pb[abnk][0:64, 0:64],
                scalar1=rk[0:64, p_:p_ + 1]), waits=[last_mm, mrk, m_az])
            ma2 = P.op("vector", lambda e, p_=p_, abnk=abnk: e.tensor_scalar_mul(
                out=attn_all[64:128, p_, 64:128], in0=pb[abnk][64:128, 64:128],
                scalar1=rk[64:128, p_:p_ + 1]), waits=[last_mm, mrk])
            bank[abnk].free = [ma1, ma2]
            m_attn[(s, p_)] = [ma1, ma2]

    def phase3(s):
        qc_res = [Res(), Res()]
        rc_res = [Res(), Res()]
        h1_res = [Res(), Res()]
        hnst_free = []
        hn_prev_act = []
        attn_deps = [m for p_ in range(8) for m in m_attn[(s, p_)]]
        pending_hn = [None]

        def flush_hn():
            if pending_hn[0] is not None:
                pending_hn[0]()
                pending_hn[0] = None

        def make_hn_flush(c, mhn):
            hsl = c % 2
            csl = c % 2

            def emit():
                nonlocal hnst_free, hn_prev_act
                ths = []
                for half in range(2):
                    bnk = half
                    mlast = None
                    for j in range(4):
                        d = half * 4 + j
                        mlast = P.op("tensor",
                                     lambda e, bnk=bnk, j=j, d=d, hsl=hsl:
                                     e.transpose(
                                         pb[bnk][:, j * 128:(j + 1) * 128],
                                         h1n_t[:, hsl, d * 128:(d + 1) * 128],
                                         ident[:]),
                                     waits=[mhn] + bank[bnk].free
                                     + (hnst_free if c % 2 == 0 else []),
                                     inc=(j == 3))
                    ths.append(mlast)
                mc0 = P.op("scalar", lambda e, hsl=hsl: e.activation(
                    out=h1n_t[:, hsl, 0:512], in_=pb[0][:], func=AF.Copy),
                    waits=[ths[0]])
                mc1 = P.op("vector", lambda e, hsl=hsl: e.tensor_copy(
                    out=h1n_t[:, hsl, 512:1024], in_=pb[1][:]), waits=[ths[1]])
                mhi3 = P.op("scalar", lambda e, csl=csl, hsl=hsl: e.activation(
                    out=hnT_hi_st.ap()[:, :, csl * 128:(csl + 1) * 128],
                    in_=h1n_t.ap()[:, hsl, :].rearrange("p (a b) -> p a b",
                                                        b=128),
                    func=AF.Copy), waits=[mc0, mc1])
                mlo3 = P.op("vector", lambda e, csl=csl, hsl=hsl: e.tensor_sub(
                    out=hnT_lo_st.ap()[:, :, csl * 128:(csl + 1) * 128],
                    in0=h1n_t.ap()[:, hsl, :].rearrange("p (a b) -> p a b",
                                                        b=128),
                    in1=hnT_hi_st.ap()[:, :, csl * 128:(csl + 1) * 128]),
                    waits=[mhi3])
                bank[0].free = [mc0]
                bank[1].free = [mc1]
                hn_prev_act = [mhi3, mlo3]
                if c % 2 == 1:
                    tb = s * T + (c - 1) * 128
                    mh_s = P.dma("gpsimd",
                                 s_hnT_hi[:, :, tb:tb + 256].transpose(
                                     [1, 0, 2]),
                                 hnT_hi_st[:], waits=[mhi3, mlo3])
                    ml_s = P.dma("gpsimd",
                                 s_hnT_lo[:, :, tb:tb + 256].transpose(
                                     [1, 0, 2]),
                                 hnT_lo_st[:], waits=[mhi3, mlo3])
                    hnst_free = [mh_s, ml_s]
                    m_hnT_store.append([mh_s, ml_s])
            return emit

        for c in range(CPS):
            sl = c % 2
            tok0 = s * T + c * 128
            mq = P.dma("sync", qT_c[:, sl, :, :],
                       s_qT[s, :, :, c * 128:(c + 1) * 128].transpose(
                           [1, 0, 2]),
                       waits=[m_qTstore[(s, e_, c // 4)] for e_ in range(8)]
                       + qc_res[sl].free)
            mr = P.dma("sync", r_c[:, sl, :], s_r[s, c * 128:(c + 1) * 128, :],
                       waits=[m_rstore[(s, c)]] + rc_res[sl].free)
            b0 = 4 if sl == 0 else 6
            sqm = None
            for p_ in range(8):
                sqm = P.op("tensor", lambda e, p_=p_, sl=sl: e.matmul(
                    pb[2][:, p_ * 2:(p_ + 1) * 2], qT_c[:, sl, p_, :],
                    ones2[:, :], start=True, stop=True),
                    waits=([mq] + bank[2].free) if p_ == 0 else (),
                    inc=(p_ == 7))
            msqe = P.op("vector", lambda e: e.tensor_copy(
                out=stat[:, 16:32], in_=pb[2][:, 0:16]), waits=[sqm])
            bank[2].free = [msqe]
            mrq = P.op("vector", lambda e, c=c: e.reciprocal(
                out=rq_all[:, c, :], in_=stat[:, 16:32]), waits=[msqe])
            yms = []
            for p_ in range(8):
                bnk = b0 + p_ // 4
                mmy = P.op("tensor", lambda e, p_=p_, bnk=bnk, sl=sl: e.matmul(
                    pb[bnk][:, (p_ % 4) * 128:(p_ % 4 + 1) * 128],
                    qT_c[:, sl, p_, :], attn_all[:, p_, :],
                    start=True, stop=True),
                    waits=[mq] + attn_deps
                    + (bank[bnk].free if p_ % 4 == 0 else []),
                    inc=(p_ % 4 == 3))
                if p_ % 4 == 3:
                    yms.append(mmy)
            flush_hn()
            hevs = []
            for p_ in range(8):
                bnk = b0 + p_ // 4
                for hh in range(2):
                    head = 2 * p_ + hh
                    cs = slice(p_ * 128 + hh * 64, p_ * 128 + (hh + 1) * 64)
                    ps = slice((p_ % 4) * 128 + hh * 64,
                               (p_ % 4) * 128 + (hh + 1) * 64)
                    mh = P.op("vector",
                              lambda e, bnk=bnk, cs=cs, ps=ps, head=head, c=c,
                              sl=sl: e.scalar_tensor_tensor(
                                  out=h1_st[:, sl, cs], in0=pb[bnk].ap()[:, ps],
                                  scalar=rq_all[:, c, head:head + 1],
                                  in1=r_c[:, sl, cs], op0=OP.mult, op1=OP.add),
                              waits=[yms[p_ // 4], mr, mrq]
                              + h1_res[sl].free)
                    hevs.append(mh)
            bank[b0].free = hevs[:8]
            bank[b0 + 1].free = hevs[8:]
            rc_res[sl].free = [hevs[-1]]
            qc_res[sl].free = yms[:] + [sqm]
            mh1 = P.dma("gpsimd", s_h1[tok0:tok0 + 128, :], h1_st[:, sl, :],
                        waits=hevs)
            m_h1store[(s, c)] = mh1
            h1_res[sl].free = [mh1]

            mrc3 = ln_stats(h1_st.ap()[:, sl, :], hevs, 0, 4)
            mnb3 = P.op("vector", lambda e: e.tensor_scalar(
                out=stat[:, 10:11], in0=stat[:, 0:1], scalar1=stat[:, 4:5],
                scalar2=-1.0, op0=OP.mult, op1=OP.mult), waits=[mrc3])
            mhn = P.op("scalar", lambda e, sl=sl, c=c: e.activation(
                out=h1n_t[:, c % 2, :], in_=h1_st[:, sl, :], func=AF.Identity,
                bias=stat[:, 10:11], scale=stat[:, 4:5]),
                waits=[mnb3] + hn_prev_act)
            pending_hn[0] = make_hn_flush(c, mhn)
        flush_hn()

    stages = []
    for s in range(NSEQ):
        stages += [(f"p1s{s}", phase1), (f"p2s{s}", phase2), (f"p3s{s}", phase3)]
    early_ffn = {"m": None}

    def emit_early_ffn_loads():
        deps = [m_ph1end[1]]
        mw = {}
        for d in range(8):
            for half in range(2):
                rows = slice(half * 1536, (half + 1) * 1536)
                mw[("g", d, half)] = P.dma(
                    "scalar", wgT[:, d, rows], s_wg[rows, d * 128:(d + 1) * 128],
                    waits=deps + m_cast["g"], transpose=True)
                mw[("u", d, half)] = P.dma(
                    "scalar", wuT[:, d, rows], s_wu[rows, d * 128:(d + 1) * 128],
                    waits=m_cast["u"] + deps, transpose=True)
        early_ffn["m"] = mw

    ffn_pre = {}

    def prefetch_ffn_blocks():
        (ek_res2, kTe_res2, ekq_res2, vp_res2, _, _) = ph2_state[1]
        ph2done = (vp_res2[0].free + vp_res2[1].free
                   + kTe_res2[0].free + kTe_res2[1].free)
        for b in range(2):
            t0b = b * 256
            mhi = P.dma("sync", hnT_hi_b[:, b % 2, :, :],
                        s_hnT_hi[:, :, t0b:t0b + 256].transpose([1, 0, 2]),
                        waits=m_hnT_store[b] + ph2done)
            mlo = P.dma("sync", hnT_lo_b[:, b % 2, :, :],
                        s_hnT_lo[:, :, t0b:t0b + 256].transpose([1, 0, 2]),
                        waits=m_hnT_store[b] + ph2done)
            ffn_pre[b] = [mhi, mlo]

    if upto != "all":
        emit_casts()
    done = upto == "prologue"
    for nm, fn in stages:
        if done:
            break
        fn(int(nm[-1]))
        if nm.startswith("p1") and upto == "all":
            pre = P.fence_markers()
            phase2_prefetch(int(nm[-1]))
            P.fence(pre)
        elif nm == "p2s1" and upto == "all":
            pre = P.fence_markers()
            emit_casts()
            emit_early_ffn_loads()
            prefetch_ffn_blocks()
            P.fence(pre)
        elif nm.startswith("p2"):
            pass
        else:
            P.fence()
        if nm == upto:
            done = True

    if done:
        P.op("gpsimd", lambda e: e.nop(),
             waits=P.queue_tail("gpsimd") + P.queue_tail("sync"), inc=False)
        with nc.Block() as block:
            P.emit(block)
        return nc
    mw = early_ffn["m"]
    m_wg = {(d, h): mw[("g", d, h)] for d in range(8) for h in range(2)}
    m_wu = {(d, h): mw[("u", d, h)] for d in range(8) for h in range(2)}
    m_wd = {}
    for hc in range(24):
        m_wd[hc] = P.dma(
            "sync", wdT_t[:, hc, :], s_wd[0:D, hc * 128:(hc + 1) * 128],
            waits=m_cast["d"], transpose=True)

    hb_res = [Res(), Res()]
    h1b_res = [Res() for _ in range(4)]
    g1_res = [Res(), Res()]
    hm_res = [Res(), Res()]
    out_res = [Res(), Res()]

    NBLK = TOK // 256
    hm_res4 = [Res() for _ in range(4)]
    for b in range(NBLK):
        bsl = b % 2
        t0 = b * 256
        if b in ffn_pre:
            mhi, mlo = ffn_pre[b]
        else:
            mhi = P.dma("sync", hnT_hi_b[:, bsl, :, :],
                        s_hnT_hi[:, :, t0:t0 + 256].transpose([1, 0, 2]),
                        waits=m_hnT_store[b] + hb_res[bsl].free)
            mlo = P.dma("sync", hnT_lo_b[:, bsl, :, :],
                        s_hnT_lo[:, :, t0:t0 + 256].transpose([1, 0, 2]),
                        waits=m_hnT_store[b] + hb_res[bsl].free)
        mh1l = []
        for tc in range(2):
            hsl = (2 * b + tc) % 4
            sseq = t0 // T
            cch = (t0 % T) // 128 + tc
            ml = P.dma("sync", h1_b[:, hsl, :],
                       s_h1[t0 + tc * 128:t0 + (tc + 1) * 128, :],
                       waits=[m_h1store[(sseq, cch)]] + h1b_res[hsl].free)
            mh1l.append(ml)

        down_banks = [4, 5] if bsl == 0 else [6, 7]
        dn_last = {}
        pending_dn = [None]

        def flush_dn():
            if pending_dn[0] is not None:
                pending_dn[0]()
                pending_dn[0] = None

        def make_dn(hc, hsl4, mhh, mhl, down_banks=down_banks, dn_last=dn_last):
            def emit():
                dlast = None
                for tc in range(2):
                    for (hmt, mdep) in ((hmT_hi, mhh), (hmT_lo, mhl)):
                        for half in range(2):
                            bnk = (down_banks if tc == 0
                                   else [down_banks[0] ^ 2,
                                         down_banks[1] ^ 2])[half]
                            st = (hc == 0 and hmt is hmT_hi)
                            sp = (hc == 23 and hmt is hmT_lo)
                            dlast = P.op(
                                "tensor",
                                lambda e, hmt=hmt, hsl4=hsl4, tc=tc, hc=hc,
                                half=half, bnk=bnk, st=st, sp=sp: e.matmul(
                                    pb[bnk][:],
                                    hmt[:, hsl4, tc * 128:(tc + 1) * 128],
                                    wdT(hc)[:, half * 512:(half + 1) * 512],
                                    start=st, stop=sp),
                                waits=[mdep, m_wd[hc]]
                                + (bank[bnk].free if hc == 0 else []),
                                inc=(hc == 23))
                            if hc == 23:
                                dn_last[(tc, half)] = dlast
                hm_res4[hsl4].free = [dlast]
            return emit

        for hc in range(24):
            gsl = hc % 2
            hsl4 = hc % 4
            gb = gsl
            ub = 2 + gsl
            wdeps = ([m_wg[(d, h)] for d in range(8) for h in range(2)]
                     + [m_wu[(d, h)] for d in range(8) for h in range(2)]) \
                if (b == 0 and hc == 0) else []
            n = 0
            mgu = None
            for d in range(8):
                for wtt, bnk in ((wgT, gb), (wuT, ub)):
                    for xt in (hnT_hi_b, hnT_lo_b):
                        n += 1
                        st = (d == 0 and xt is hnT_hi_b)
                        sp = (d == 7 and xt is hnT_lo_b)
                        mgu = P.op("tensor",
                                   lambda e, d=d, wtt=wtt, bnk=bnk, xt=xt,
                                   hc=hc, bsl=bsl, st=st, sp=sp: e.matmul(
                                       pb[bnk][:, 0:256],
                                       wtt[:, d, hc * 128:(hc + 1) * 128],
                                       xt.ap()[:, bsl, d, :],
                                       start=st, stop=sp),
                                   waits=(wdeps + [mhi, mlo] + bank[gb].free
                                          + bank[ub].free) if n == 1 else (),
                                   inc=(n == 32))
            flush_dn()
            mg1 = P.op("scalar", lambda e, gb=gb, gsl=gsl: e.activation(
                out=g1_t[:, gsl, :], in_=pb[gb][:, 0:256], func=AF.Silu,
                scale=QS), waits=[mgu] + g1_res[gsl].free)
            mhm = P.op("vector",
                       lambda e, ub=ub, gsl=gsl: e.scalar_tensor_tensor(
                           out=g1_t[:, gsl, :], in0=pb[ub][:, 0:256], scalar=QS,
                           in1=g1_t[:, gsl, :], op0=OP.mult, op1=OP.mult),
                       waits=[mg1])
            bank[gb].free = [mhm]
            bank[ub].free = [mhm]
            mhh = P.op("vector", lambda e, gsl=gsl, hsl4=hsl4: e.tensor_copy(
                out=hmT_hi[:, hsl4, :], in_=g1_t[:, gsl, :]),
                waits=[mhm] + hm_res4[hsl4].free)
            mhl = P.op("vector", lambda e, gsl=gsl, hsl4=hsl4: e.tensor_sub(
                out=hmT_lo[:, hsl4, :], in0=g1_t[:, gsl, :],
                in1=hmT_hi[:, hsl4, :]), waits=[mhh])
            g1_res[gsl].free = [mhl]
            pending_dn[0] = make_dn(hc, hsl4, mhh, mhl)
        flush_dn()

        for tc in range(2):
            osl = tc
            hsl = (2 * b + tc) % 4
            oevs = []
            for half in range(2):
                bnk = (down_banks if tc == 0
                       else [down_banks[0] ^ 2, down_banks[1] ^ 2])[half]
                mo = P.op("vector",
                          lambda e, bnk=bnk, osl=osl, hsl=hsl, half=half:
                          e.scalar_tensor_tensor(
                              out=out_stg[:, osl, half * 512:(half + 1) * 512],
                              in0=pb[bnk][:], scalar=QS,
                              in1=h1_b.ap()[:, hsl,
                                            half * 512:(half + 1) * 512],
                              op0=OP.mult, op1=OP.add),
                          waits=[dn_last[(tc, half)], mh1l[tc]]
                          + out_res[osl].free)
                bank[bnk].free = [mo]
                oevs.append(mo)
            most = P.dma("gpsimd",
                         out_hbm[t0 + tc * 128:t0 + (tc + 1) * 128, :],
                         out_stg[:, osl, :], waits=oevs)
            out_res[osl].free = [most]
            h1b_res[hsl].free = [most]
        hb_res[bsl].free = [dn_last[(1, 1)]]

    P.op("gpsimd", lambda e: e.nop(),
         waits=P.queue_tail("gpsimd") + P.queue_tail("sync")
         + P.queue_tail("scalar"), inc=False)

    with nc.Block() as block:
        P.emit(block)
    return nc


_NC_CACHE = {}


def kernel(**inputs):
    x = np.ascontiguousarray(np.asarray(inputs["L0_attn_x"], dtype=np.float32))
    B = x.shape[0]
    per = B // NCORES
    key = "main"
    if key not in _NC_CACHE:
        _NC_CACHE[key] = build()
    nc = _NC_CACHE[key]
    common = {
        "ln_g": np.asarray(inputs["L0_attn_ln_g"], dtype=np.float32),
        "ln_b": np.asarray(inputs["L0_attn_ln_b"], dtype=np.float32),
        "Wq": np.asarray(inputs["L0_attn_Wq"], dtype=np.float32),
        "Wk": np.asarray(inputs["L0_attn_Wk"], dtype=np.float32),
        "Wv": np.asarray(inputs["L0_attn_Wv"], dtype=np.float32),
        "bq": np.asarray(inputs["L0_attn_bq"], dtype=np.float32),
        "bk": np.asarray(inputs["L0_attn_bk"], dtype=np.float32),
        "bv": np.asarray(inputs["L0_attn_bv"], dtype=np.float32),
        "wg": np.asarray(inputs["L0_ffn_w_gate"], dtype=np.int32),
        "wu": np.asarray(inputs["L0_ffn_w_up"], dtype=np.int32),
        "wd": np.asarray(inputs["L0_ffn_w_down"], dtype=np.int32),
        "ident": np.eye(128, dtype=np.float32),
    }
    in_maps = []
    for c in range(NCORES):
        shard = np.ascontiguousarray(
            x[c * per:(c + 1) * per].reshape(per * x.shape[1], x.shape[2]))
        m = dict(common)
        m["x"] = shard
        in_maps.append(m)
    res = run_bass_kernel_spmd(nc, in_maps, core_ids=list(range(NCORES)))
    out = np.concatenate([np.asarray(r["out"]) for r in res.results], axis=0)
    return out.reshape(B, x.shape[1], x.shape[2])


# version 18
# speedup vs baseline: 1.0386x; 1.0230x over previous
import numpy as np
import concourse.bass as bass
import concourse.mybir as mybir
from concourse.bass_utils import run_bass_kernel_spmd

F32 = mybir.dt.float32
BF16 = mybir.dt.bfloat16
I32 = mybir.dt.int32
AF = mybir.ActivationFunctionType
AX = mybir.AxisListType
OP = mybir.AluOpType

NCORES = 8
T = 2048
D = 1024
HID = 3072
NSEQ = 2
TOK = NSEQ * T
CPS = T // 128
EPS = 1e-5
QS = 1.0 / 127.0
KB = 1024


class Marker:
    __slots__ = ("sem", "val")

    def __init__(self, sem, val):
        self.sem = sem
        self.val = val


class Prog:

    ENGINES = ("sync", "gpsimd", "vector", "scalar", "tensor")

    KPOOL = {"s": 14, "g": 14, "a": 48}

    def __init__(self, nc):
        self.nc = nc
        self.ops = {e: [] for e in self.ENGINES}
        self.sems = {}
        for name in ("te", "dv", "ac", "sy", "gp"):
            h = nc.semaphore(name).__enter__()
            self.sems[name] = [h, 0]
        for q in ("s", "g", "a"):
            for j in range(self.KPOOL[q]):
                name = f"q{q}{j}"
                h = nc.semaphore(name).__enter__()
                self.sems[name] = [h, 0]
        self.qprefix = {"sync": "qs", "gpsimd": "qg", "scalar": "qa"}
        self.qcount = {"sync": 0, "gpsimd": 0, "scalar": 0}
        self.csem = {"vector": "dv", "scalar": "ac", "tensor": "te",
                     "sync": "sy", "gpsimd": "gp"}
        self.pending_fence = {e: [] for e in self.ENGINES}

    def cur(self, name):
        return Marker(name, self.sems[name][1])

    def queue_tail(self, queue):
        pre = self.qprefix[queue]
        return [self.cur(f"{pre}{j}") for j in range(self.KPOOL[pre[1]])
                if self.sems[f"{pre}{j}"][1] > 0]

    def fence(self, ms=None):
        if ms is None:
            ms = self.fence_markers()
        for e in self.ENGINES:
            self.pending_fence[e] = ms

    def fence_markers(self):
        return ([self.cur(n) for n in ("te", "dv", "ac")]
                + self.queue_tail("sync") + self.queue_tail("gpsimd")
                + self.queue_tail("scalar"))

    def _take_fence(self, eng):
        f = self.pending_fence[eng]
        self.pending_fence[eng] = []
        return f

    def dma(self, queue, out, in_, waits=(), **kw):
        i = self.qcount[queue]
        self.qcount[queue] = i + 1
        name = f"{self.qprefix[queue]}{i % self.KPOOL[self.qprefix[queue][1]]}"
        prev = self.sems[name][1]
        self.sems[name][1] = prev + 16
        val = prev + 16
        h = self.sems[name][0]
        waits = list(waits)
        if prev > 0:
            waits.append(Marker(name, prev))

        def fn(e, out=out, in_=in_, kw=kw, h=h):
            return e.dma_start(out=out, in_=in_, **kw).then_inc(h, 16)

        eng = "scalar" if queue == "scalar" else queue
        self.ops[eng].append((fn, self._resolve(waits, eng), None))
        return Marker(name, val)

    AUTOCHAIN = ("vector", "scalar")

    def op(self, eng, fn, waits=(), inc=True):
        m = None
        incpair = None
        name = self.csem[eng]
        waits = list(waits)
        if eng in self.AUTOCHAIN and self.sems[name][1] > 0:
            waits.append(Marker(name, self.sems[name][1]))
        if inc:
            self.sems[name][1] += 1
            m = Marker(name, self.sems[name][1])
            incpair = (self.sems[name][0], 1)
        self.ops[eng].append((fn, self._resolve(waits, eng), incpair))
        return m

    def _resolve(self, waits, eng):
        out = []
        for w in list(waits) + self._take_fence(eng):
            if w is None:
                continue
            if isinstance(w, (list, tuple)):
                out.extend((m.sem, m.val) for m in w if m is not None)
            else:
                out.append((w.sem, w.val))
        return out

    def emit(self, block):
        def run(e, eng_name):
            own = self.csem.get(eng_name)
            skip_own = eng_name not in self.AUTOCHAIN
            ew = {}
            for fn, waits, incpair in self.ops[eng_name]:
                for sem_name, val in waits:
                    if sem_name == own and skip_own:
                        continue
                    if ew.get(sem_name, 0) >= val:
                        continue
                    e.wait_ge(self.sems[sem_name][0], val)
                    ew[sem_name] = val
                inst = fn(e)
                if incpair is not None:
                    inst.then_inc(*incpair)

        block.sync(lambda e: run(e, "sync"))
        block.gpsimd(lambda e: run(e, "gpsimd"))
        block.vector(lambda e: run(e, "vector"))
        block.scalar(lambda e: run(e, "scalar"))
        block.tensor(lambda e: run(e, "tensor"))


class Res:
    def __init__(self):
        self.ready = []
        self.free = []


def build(debug=False, upto='all', nchunks=16, sub='full'):
    nc = bass.Bass("TRN2", target_bir_lowering=False)

    x_in = nc.dram_tensor("x", [TOK, D], F32, kind="ExternalInput").ap()
    ln_g = nc.dram_tensor("ln_g", [D], F32, kind="ExternalInput").ap()
    ln_b = nc.dram_tensor("ln_b", [D], F32, kind="ExternalInput").ap()
    Wq = nc.dram_tensor("Wq", [D, D], F32, kind="ExternalInput").ap()
    Wk = nc.dram_tensor("Wk", [D, D], F32, kind="ExternalInput").ap()
    Wv = nc.dram_tensor("Wv", [D, D], F32, kind="ExternalInput").ap()
    bq = nc.dram_tensor("bq", [D], F32, kind="ExternalInput").ap()
    bk = nc.dram_tensor("bk", [D], F32, kind="ExternalInput").ap()
    bv = nc.dram_tensor("bv", [D], F32, kind="ExternalInput").ap()
    wg_i = nc.dram_tensor("wg", [HID, D], I32, kind="ExternalInput").ap()
    wu_i = nc.dram_tensor("wu", [HID, D], I32, kind="ExternalInput").ap()
    wd_i = nc.dram_tensor("wd", [D, HID], I32, kind="ExternalInput").ap()
    ident_in = nc.dram_tensor("ident", [128, 128], F32, kind="ExternalInput").ap()
    out_hbm = nc.dram_tensor("out", [TOK, D], F32, kind="ExternalOutput").ap()

    ik = "ExternalOutput" if debug else "Internal"
    s_wg = nc.dram_tensor("s_wg", [HID, D], BF16, kind="Internal").ap()
    s_wu = nc.dram_tensor("s_wu", [HID, D], BF16, kind="Internal").ap()
    s_wd = nc.dram_tensor("s_wd", [D, HID], BF16, kind="Internal").ap()
    s_kT = nc.dram_tensor("s_kT", [NSEQ, 8, 128, T], F32, kind=ik).ap()
    s_v = nc.dram_tensor("s_v", [NSEQ, T, D], F32, kind=ik).ap()
    s_qT = nc.dram_tensor("s_qT", [NSEQ, 8, 128, T], F32, kind=ik).ap()
    s_r = nc.dram_tensor("s_r", [NSEQ, T, D], F32, kind=ik).ap()
    s_h1 = nc.dram_tensor("s_h1", [TOK, D], F32, kind=ik).ap()
    s_hnT_hi = nc.dram_tensor("s_hnT_hi", [8, 128, TOK], BF16, kind="Internal").ap()
    s_hnT_lo = nc.dram_tensor("s_hnT_lo", [8, 128, TOK], BF16, kind="Internal").ap()
    s_brow_k = nc.dram_tensor("s_brow_k", [D], F32, kind="Internal").ap()

    arena = nc.alloc_sbuf_tensor("arena", [128, 192 * KB], mybir.dt.uint8)
    _base = nc.lookup_mloc(arena).addr

    def A(name, shape, dtype, offset):
        return nc.alloc_sbuf_tensor_at(name, shape, dtype, offset=_base + offset)
    g_col = A("g_col", [128, 8], F32, offset=0)
    bcol_hi = A("bcol_hi", [128, 8], BF16, offset=64)
    bcol_lo = A("bcol_lo", [128, 8], BF16, offset=96)
    bk_col = A("bk_col", [128, 8], F32, offset=128)
    Mk = A("Mk", [128, 8], F32, offset=192)
    rk = A("rk", [128, 8], F32, offset=224)
    nMk = A("nMk", [128, 8], F32, offset=256)
    bcol_f = A("bcol_f", [128, 8], F32, offset=288)
    rq_all = A("rq_all", [128, CPS, 16], F32, offset=320)
    stat = A("stat", [128, 48], F32, offset=1344)
    ones_t = A("ones_t", [128, 128], F32, offset=1536)
    ident = A("ident", [128, 128], F32, offset=2048)
    ones_bf = A("ones_bf", [1, 128], BF16, offset=2560)
    bq_col = A("bq_col", [128, 8], F32, offset=2816)
    ones2 = A("ones2", [128, 2], F32, offset=2880)
    brow_v_hi = A("brow_v_hi", [1, D], BF16, offset=2816 + 4 * KB)
    brow_v_lo = A("brow_v_lo", [1, D], BF16, offset=2816 + 6 * KB)
    attn_all = A("attn_all", [128, 8, 128], F32, offset=2816 + 8 * KB)

    WOFF = 16 * KB
    wT = {}
    for i, (w, ver) in enumerate([(w, v) for w in "qkv" for v in ("hi", "lo")]):
        wT[(w, ver)] = A(f"w{w}T_{ver}", [128, 8, D], BF16, offset=WOFF + i * 16 * KB)

    P1 = 112 * KB
    x0_r = A("x0_r", [128, 2, D], F32, offset=P1)
    n1_t = A("n1_t", [128, D], F32, offset=P1 + 8 * KB)
    xn_r = A("xn_r", [128, 2, D], F32, offset=P1 + 12 * KB)
    r_r = A("r_r", [128, 2, D], F32, offset=P1 + 20 * KB)
    qT_stg = A("qT_stg", [128, 2, 512], F32, offset=P1 + 44 * KB)
    kT_stg = A("kT_stg", [128, 2, 512], F32, offset=P1 + 52 * KB)
    v_stg = A("v_stg", [128, 2, D], F32, offset=P1 + 56 * KB)
    xnT_hi = A("xnT_hi", [128, 8, 4, 128], BF16, offset=P1 + 64 * KB)
    xnT_lo = A("xnT_lo", [128, 8, 4, 128], BF16, offset=P1 + 72 * KB)

    kTe2 = A("kTe2", [128, 2, T], F32, offset=P1)
    ekTq = A("ekTq", [128, 2, 512], F32, offset=P1 + 16 * KB)
    ek_r = A("ek_r", [128, 2, 4, 128], F32, offset=P1 + 20 * KB)
    v_p2 = A("v_p2", [128, 2, 16, 128], F32, offset=P1 + 24 * KB)

    qT_c = A("qT_c", [128, 2, 8, 128], F32, offset=P1 + 40 * KB)
    r_c = A("r_c", [128, 2, D], F32, offset=P1 + 48 * KB)
    h1_st = A("h1_st", [128, 2, D], F32, offset=P1 + 56 * KB)
    h1n_t = A("h1n_t", [128, 2, D], F32, offset=P1 + 64 * KB)
    hnT_hi_st = A("hnT_hi_st", [128, 8, 256], BF16, offset=P1 + 72 * KB)
    hnT_lo_st = A("hnT_lo_st", [128, 8, 256], BF16, offset=P1 + 76 * KB)

    pw_f = A("pw_f", [128, 4, D], F32, offset=P1 + 28 * KB)
    ptmp = A("ptmp", [128, 2, D], F32, offset=P1 + 44 * KB)
    brow = {w: A(f"brow_{w}", [1, D], F32, offset=P1 + 52 * KB + i * 4 * KB)
            for i, w in enumerate("qkv")}

    wgT = A("wgT", [128, 8, HID], BF16, offset=16 * KB)
    wuT = A("wuT", [128, 8, HID], BF16, offset=64 * KB)
    hnT_hi_b = A("hnT_hi_b", [128, 2, 8, 256], BF16, offset=112 * KB)
    hnT_lo_b = A("hnT_lo_b", [128, 2, 8, 256], BF16, offset=120 * KB)
    wdT_t = A("wdT_t", [128, 24, D], BF16, offset=128 * KB)
    g1_t = A("g1_t", [128, 2, 256], F32, offset=176 * KB)
    hmT_hi = A("hmT_hi", [128, 4, 256], BF16, offset=178 * KB)
    hmT_lo = A("hmT_lo", [128, 4, 256], BF16, offset=180 * KB)
    out_stg = A("out_stg", [128, 2, D], F32, offset=182 * KB)
    h1_b = A("h1_b", [128, 4, D], F32, offset=0)

    def wdT(hc):
        return wdT_t.ap()[:, hc, :]

    pb = [nc.alloc_psum_tensor(f"pb{i}", [128, 512], F32) for i in range(8)]

    P = Prog(nc)
    bank = [Res() for _ in range(8)]

    m_ident = P.dma("sync", ident[:], ident_in[:, :])
    m_g_row = P.dma("sync", pw_f[0:1, 0, :], ln_g[:])
    m_b_row = P.dma("sync", pw_f[0:1, 1, :], ln_b[:])
    m_bvec = {}
    for w, src in (("q", bq), ("k", bk), ("v", bv)):
        m_bvec[w] = P.dma("sync", brow[w][0:1, :], src[:])
    pre_x0 = {}
    for c0 in range(2):
        pre_x0[c0] = P.dma("sync", x0_r[:, c0, :],
                           x_in[c0 * 128:(c0 + 1) * 128, :])
    m_ones = P.op("vector", lambda e: e.memset(ones_t[0:1, :], 1.0))
    m_onesb = P.op("vector", lambda e: e.memset(ones_bf[0:1, :], 1.0))

    def colize(dst, src_row, row_marker, bnk):
        mms = []
        for cc in range(8):
            rw = row_marker if isinstance(row_marker, list) else [row_marker]
            mms.append(P.op("tensor", lambda e, cc=cc, bnk=bnk, src_row=src_row:
                            e.matmul(pb[bnk][:, cc:cc + 1],
                                     src_row[0:1, cc * 128:(cc + 1) * 128],
                                     ones_t[0:1, 0:1], start=True, stop=True),
                            waits=rw + [m_ones] + (bank[bnk].free
                                                   if cc == 0 else []),
                            inc=(cc == 7)))
        mc = P.op("vector", lambda e, dst=dst, bnk=bnk: e.tensor_copy(
            out=dst, in_=pb[bnk][:, 0:8]), waits=[mms[-1]])
        bank[bnk].free = [mc]
        return mc

    m_gcol = colize(g_col[:], pw_f.ap()[:, 0, :], m_g_row, 6)
    m_bcolf = colize(bcol_f[:], pw_f.ap()[:, 1, :], m_b_row, 7)

    m_cast = {"g": [], "u": [], "d": []}
    cast_slices = []
    for j in range(8):
        cast_slices.append(("g", s_wg[j * 384:(j + 1) * 384, :],
                            wg_i[j * 384:(j + 1) * 384, :]))
    for j in range(8):
        cast_slices.append(("u", s_wu[j * 384:(j + 1) * 384, :],
                            wu_i[j * 384:(j + 1) * 384, :]))
    for j in range(8):
        cast_slices.append(("d", s_wd[j * 128:(j + 1) * 128, :],
                            wd_i[j * 128:(j + 1) * 128, :]))

    def pump_cast():
        if cast_slices:
            w, dst, srcq = cast_slices.pop(0)
            m_cast[w].append(P.dma("gpsimd", dst, srcq))

    def emit_casts():
        while cast_slices:
            pump_cast()

    pw_free = [[m_gcol], [m_bcolf], [], []]
    ptmp_free = [[], [], [], []]
    m_wT = {}
    for wi, (w, wsrc) in enumerate((("v", Wv), ("q", Wq), ("k", Wk))):
        for rb in range(8):
            sl = (wi * 8 + rb) % 4
            rows = slice(rb * 128, (rb + 1) * 128)
            mf = P.dma("sync", pw_f[:, sl, :],
                       wsrc[rows, :], waits=pw_free[sl])
            trs = []
            for half in range(2):
                bnk = 6 + half
                mlast = None
                for j in range(4):
                    d = half * 4 + j
                    mlast = P.op("tensor",
                                 lambda e, bnk=bnk, j=j, d=d, sl=sl:
                                 e.transpose(
                                     pb[bnk][:, j * 128:(j + 1) * 128],
                                     pw_f[:, sl, d * 128:(d + 1) * 128],
                                     ident[:]),
                                 waits=[mf, m_ident] + bank[bnk].free,
                                 inc=(j == 3))
                trs.append(mlast)
            pw_free[sl] = trs[:]
            psl2 = sl % 2
            mtc0 = P.op("scalar", lambda e, psl2=psl2: e.activation(
                out=ptmp[:, psl2, 0:512], in_=pb[6][:], func=AF.Copy),
                waits=[trs[0]] + ptmp_free[psl2])
            mtc1 = P.op("vector", lambda e, psl2=psl2: e.tensor_copy(
                out=ptmp[:, psl2, 512:1024], in_=pb[7][:]),
                waits=[trs[1]] + ptmp_free[psl2])
            bank[6].free = [mtc0]
            bank[7].free = [mtc1]
            mhiw = P.op("scalar", lambda e, psl2=psl2, w=w, rb=rb: e.activation(
                out=wT[(w, "hi")].ap()[:, :, rb * 128:(rb + 1) * 128],
                in_=ptmp.ap()[:, psl2, :].rearrange("p (a b) -> p a b", b=128),
                func=AF.Copy), waits=[mtc0, mtc1])
            mlow = P.op("vector", lambda e, psl2=psl2, w=w, rb=rb: e.tensor_sub(
                out=wT[(w, "lo")].ap()[:, :, rb * 128:(rb + 1) * 128],
                in0=ptmp.ap()[:, psl2, :].rearrange("p (a b) -> p a b", b=128),
                in1=wT[(w, "hi")].ap()[:, :, rb * 128:(rb + 1) * 128]),
                waits=[mhiw])
            ptmp_free[psl2] = [mlow]
            m_wT[(w, rb)] = [mhiw, mlow]

    m_bch = P.op("scalar", lambda e: e.activation(
        out=bcol_hi[:], in_=bcol_f[:], func=AF.Copy), waits=[m_bcolf])
    m_bcl = P.op("vector", lambda e: e.tensor_sub(
        out=bcol_lo[:], in0=bcol_f[:], in1=bcol_hi[:]), waits=[m_bcolf, m_bch])
    m_az = P.op("vector", lambda e: e.memset(attn_all[:], 0.0))

    m_brow = {}
    for w in "qkv":
        wdeps = [m for rb in range(8) for m in m_wT[(w, rb)]]
        lasts = []
        for half in range(2):
            hs = slice(half * 512, (half + 1) * 512)
            n = 0
            last = None
            for d in range(8):
                for (bc, wv) in ((bcol_hi, "hi"), (bcol_lo, "hi"), (bcol_hi, "lo")):
                    n += 1
                    last = P.op("tensor", lambda e, d=d, bc=bc, wv=wv, hs=hs, w=w,
                                half=half, st=(n == 1), sp=(n == 24): e.matmul(
                        pb[half][0:1, :], bc[:, d:d + 1], wT[(w, wv)][:, d, hs],
                        start=st, stop=sp),
                        waits=([m_bch, m_bcl] + wdeps + bank[half].free)
                        if n == 1 else (),
                        inc=(n == 24))
            lasts.append(last)
        m1 = P.op("vector", lambda e, w=w: e.tensor_add(
            out=brow[w][0:1, 0:512], in0=brow[w][0:1, 0:512], in1=pb[0][0:1, :]),
            waits=[lasts[0], m_bvec[w]])
        m2 = P.op("vector", lambda e, w=w: e.tensor_add(
            out=brow[w][0:1, 512:1024], in0=brow[w][0:1, 512:1024],
            in1=pb[1][0:1, :]), waits=[lasts[1]])
        bank[0].free = [m1]
        bank[1].free = [m2]
        m_brow[w] = [m1, m2]

    m_btile = {}
    for w, (rhi, rlo) in (("v", (brow_v_hi, brow_v_lo)),):
        mh_ = P.op("scalar", lambda e, w=w, rhi=rhi: e.activation(
            out=rhi[0:1, :], in_=brow[w][0:1, :], func=AF.Copy),
            waits=m_brow[w])
        ml_ = P.op("vector", lambda e, w=w, rhi=rhi, rlo=rlo: e.tensor_sub(
            out=rlo[0:1, :], in0=brow[w][0:1, :], in1=rhi[0:1, :]),
            waits=m_brow[w] + [mh_])
        m_btile[w] = [mh_, ml_]
    mz2 = P.op("vector", lambda e: e.memset(ones2[:], 0.0))
    mo2a = P.op("vector", lambda e: e.memset(ones2[0:64, 0:1], 1.0), waits=[mz2])
    m_ones2 = P.op("vector", lambda e: e.memset(ones2[64:128, 1:2], 1.0),
                   waits=[mo2a])

    m_bkc = colize(bk_col[:], brow["k"], m_brow["k"], 6)
    m_bqc = colize(bq_col[:], brow["q"], m_brow["q"], 7)


    v_ready = ([m for rb in range(8) for m in m_wT[("v", rb)]]
               + m_btile["v"] + [m_gcol, m_ident, m_onesb])
    qk_ready = ([m for w in "qk" for rb in range(8) for m in m_wT[(w, rb)]]
                + [m_bkc, m_bqc, m_ones2])

    x0_res = [Res(), Res()]
    xn_res = [Res(), Res()]
    r_res = [Res(), Res()]
    q_res = [Res(), Res()]
    eq_res = [Res(), Res()]
    qTs_res = [Res(), Res()]
    kTs_res = [Res(), Res()]
    vs_res = [Res(), Res()]
    xnT_res = [Res() for _ in range(4)]
    n1_free = []
    m_ph1end = {}
    m_negMk = {}
    m_kstore = {}
    m_vstore = {}
    m_qTstore = {}
    m_rstore = {}
    m_h1store = {}
    m_hnT_store = []
    m_attn = {}

    def ln_stats(src_ap, dep_markers, mean_slot, rstd_slot):
        bnv = stat.ap()[:, 32:44].rearrange("p (g x) -> p g x", x=6)
        mb1 = P.op("vector", lambda e: e.bn_stats(
            out=stat.ap()[:, 32:38], in_=src_ap[:, 0:512]), waits=dep_markers)
        mb2 = P.op("vector", lambda e: e.bn_stats(
            out=stat.ap()[:, 38:44], in_=src_ap[:, 512:1024]), waits=dep_markers)
        mag = P.op("vector", lambda e, ms=mean_slot: e.bn_aggr(
            out=stat[:, ms:ms + 2], in_=bnv), waits=[mb1, mb2])
        mvp = P.op("vector", lambda e, ms=mean_slot: e.tensor_scalar_add(
            out=stat[:, ms + 2:ms + 3], in0=stat[:, ms + 1:ms + 2],
            scalar1=EPS), waits=[mag])
        msr = P.op("scalar", lambda e, ms=mean_slot: e.activation(
            out=stat[:, ms + 3:ms + 4], in_=stat[:, ms + 2:ms + 3],
            func=AF.Sqrt), waits=[mvp])
        mrc = P.op("vector", lambda e, ms=mean_slot, rs=rstd_slot: e.reciprocal(
            out=stat[:, rs:rs + 1], in_=stat[:, ms + 3:ms + 4]), waits=[msr])
        return mrc

    def phase1(s):
        nonlocal n1_free
        m_mkinit = P.op("vector", lambda e: e.memset(Mk[:], -3.0e38))

        def S1(c):
            nonlocal n1_free
            tok0 = s * T + c * 128
            sl = c % 2
            x0 = x0_r[:, sl, :]
            if s == 0 and c < 2:
                mx = pre_x0[c]
            else:
                mx = P.dma("sync", x0, x_in[tok0:tok0 + 128, :],
                           waits=x0_res[sl].free)
            mrc1 = ln_stats(x0, [mx] + n1_free, 0, 4)
            mnb1 = P.op("vector", lambda e: e.tensor_scalar(
                out=stat[:, 10:11], in0=stat[:, 0:1], scalar1=stat[:, 4:5],
                scalar2=-1.0, op0=OP.mult, op1=OP.mult), waits=[mrc1])
            mn1 = P.op("scalar", lambda e, x0=x0: e.activation(
                out=n1_t[:], in_=x0, func=AF.Identity,
                bias=stat[:, 10:11], scale=stat[:, 4:5]),
                waits=[mnb1] + n1_free)
            mr = P.op("vector", lambda e, x0=x0, sl=sl: e.tensor_add(
                out=r_r[:, sl, :], in0=x0, in1=n1_t[:]),
                waits=[mn1] + r_res[sl].free)
            x0_res[sl].free = [mr]
            mrst = P.dma("gpsimd", s_r[s, c * 128:(c + 1) * 128, :],
                         r_r[:, sl, :], waits=[mr])
            r_res[sl].free = [mrst]
            m_rstore[(s, c)] = mrst
            mrc2 = ln_stats(n1_t.ap(), [mn1], 5, 9)
            mnb2 = P.op("vector", lambda e: e.tensor_scalar(
                out=stat[:, 11:12], in0=stat[:, 5:6], scalar1=stat[:, 9:10],
                scalar2=-1.0, op0=OP.mult, op1=OP.mult), waits=[mrc2])
            mxn = P.op("scalar", lambda e, sl=sl: e.activation(
                out=xn_r[:, sl, :], in_=n1_t[:], func=AF.Identity,
                bias=stat[:, 11:12], scale=stat[:, 9:10]),
                waits=[mnb2] + xn_res[sl].free)
            xn_res[sl].ready = [mxn]
            return {"mxn": mxn}

        def S2(c, st1):
            nonlocal n1_free
            sl = c % 2
            tsl = c % 4
            mxn = st1["mxn"]
            tms = []
            for half in range(2):
                bnk = half
                mlast = None
                for j in range(4):
                    d = half * 4 + j
                    mlast = P.op("tensor",
                                 lambda e, bnk=bnk, j=j, d=d, sl=sl:
                                 e.transpose(
                                     pb[bnk][:, j * 128:(j + 1) * 128],
                                     xn_r[:, sl, d * 128:(d + 1) * 128],
                                     ident[:]),
                                 waits=[mxn, m_ident] + bank[bnk].free
                                 + xnT_res[tsl].free,
                                 inc=(j == 3))
                tms.append(mlast)
            xn_res[sl].free = tms[:]
            evac0 = []
            for j in range(4):
                d = j
                evac0.append(P.op("scalar", lambda e, j=j, d=d: e.activation(
                    out=n1_t[:, d * 128:(d + 1) * 128],
                    in_=pb[0][:, j * 128:(j + 1) * 128],
                    func=AF.Copy, scale=g_col[:, d:d + 1]),
                    waits=[tms[0], m_gcol]))
            evac1 = []
            for j in range(4):
                d = 4 + j
                evac1.append(P.op("vector",
                                  lambda e, j=j, d=d: e.tensor_scalar_mul(
                                      out=n1_t[:, d * 128:(d + 1) * 128],
                                      in0=pb[1][:, j * 128:(j + 1) * 128],
                                      scalar1=g_col[:, d:d + 1]),
                                  waits=[tms[1]]))
            mhi = P.op("scalar", lambda e, tsl=tsl: e.activation(
                out=xnT_hi.ap()[:, :, tsl, :], in_=n1_t[:], func=AF.Copy),
                waits=evac0 + evac1)
            mlo = P.op("vector", lambda e, tsl=tsl: e.tensor_sub(
                out=xnT_lo.ap()[:, :, tsl, :], in0=n1_t[:],
                in1=xnT_hi.ap()[:, :, tsl, :]), waits=[mhi])
            evs = [mhi, mlo]
            bank[0].free = [mhi] + evac0
            bank[1].free = [mhi, mlo] + evac1
            xnT_res[tsl].ready = evs
            n1_free = [mlo]
            return {"evs": evs}

        def S3(c, st2):
            tsl = c % 4
            n = 0
            mhi_m, mlo_m = st2["evs"]
            w0 = (v_ready + [mhi_m] + bank[4].free + bank[5].free)
            for (xt, wver, movs) in (
                (xnT_hi, "hi", (("v", "hi"), ("v", "lo"))),
                (xnT_lo, "lo", (("v", "hi"),)),
            ):
                for d in range(8):
                    for (w, mv) in movs:
                        for half in range(2):
                            n += 1
                            bnk = 4 + half
                            st = (d == 0 and wver == "hi" and mv == "hi")
                            P.op("tensor",
                                 lambda e, bnk=bnk, d=d, xt=xt, tsl=tsl,
                                 mv=mv, half=half, st=st: e.matmul(
                                     pb[bnk][:], xt[:, d, tsl, :],
                                     wT[("v", mv)][:, d,
                                                   half * 512:(half + 1) * 512],
                                     start=st, stop=False),
                                 waits=w0 if n == 1
                                 else ([mlo_m] if (wver == "lo" and d == 0
                                                   and half == 0) else ()),
                                 inc=False)
            mqv = None
            for half in range(2):
                hs = slice(half * 512, (half + 1) * 512)
                bnk = 4 + half
                P.op("tensor", lambda e, bnk=bnk, hs=hs:
                     e.matmul(pb[bnk][:], ones_bf[0:1, 0:128],
                              brow_v_hi[0:1, hs], start=False, stop=False),
                     inc=False)
                mqv = P.op("tensor", lambda e, bnk=bnk, hs=hs:
                           e.matmul(pb[bnk][:], ones_bf[0:1, 0:128],
                                    brow_v_lo[0:1, hs], start=False,
                                    stop=True),
                           inc=(half == 1))
            return {"mqv": mqv}

        def S4(c, st3):
            mqv = st3["mqv"]
            qsl = c % 2
            mve1 = P.op("scalar", lambda e, qsl=qsl: e.activation(
                out=v_stg[:, qsl, 0:512], in_=pb[4][:], func=AF.Copy),
                waits=[mqv] + vs_res[qsl].free)
            mve2 = P.op("vector", lambda e, qsl=qsl: e.tensor_copy(
                out=v_stg[:, qsl, 512:1024], in_=pb[5][:]),
                waits=[mqv] + vs_res[qsl].free)
            bank[4].free = [mve1]
            bank[5].free = [mve2]
            mvs = P.dma("gpsimd", s_v[s, c * 128:(c + 1) * 128, :],
                        v_stg[:, qsl, :], waits=[mve1, mve2])
            vs_res[qsl].free = [mvs]
            m_vstore[(s, c)] = mvs

        def kblock(blk):
            ktok = slice(blk * 512, (blk + 1) * 512)
            kdeps = [m for t4 in range(4) for m in xnT_res[t4].ready] \
                + (qk_ready if blk == 0 else [])
            klast = None
            for e_ in range(8):
                for (wname, bbase, res, stg, stores, mcol) in (
                    ("q", 2, qTs_res, qT_stg, m_qTstore, m_bqc),
                    ("k", 6, kTs_res, kT_stg, m_kstore, m_bkc),
                ):
                    bnk = bbase + (e_ % 2)
                    n = 0
                    last = None
                    for d in range(8):
                        for (wver, xt) in (("hi", xnT_hi), ("hi", xnT_lo),
                                           ("lo", xnT_hi)):
                            n += 1
                            st = (n == 1)
                            sp = (n == 24)
                            last = P.op("tensor",
                                        lambda e, bnk=bnk, d=d, e_=e_, xt=xt,
                                        wver=wver, st=st, sp=sp, wname=wname:
                                        e.matmul(
                                            pb[bnk][:],
                                            wT[(wname, wver)][
                                                :, d, e_ * 128:(e_ + 1) * 128],
                                            xt.ap()[:, d, :, :].rearrange(
                                                "p a b -> p (a b)"),
                                            start=st, stop=sp),
                                        waits=(kdeps + bank[bnk].free)
                                        if n == 1 else (),
                                        inc=(n == 24))
                    klast = last
                    ksl = e_ % 2
                    if wname == "q":
                        mke = P.op("scalar",
                                   lambda e, bnk=bnk, ksl=ksl, e_=e_:
                                   e.activation(
                                       out=qT_stg[:, ksl, :], in_=pb[bnk][:],
                                       func=AF.Exp,
                                       bias=bq_col[:, e_:e_ + 1]),
                                   waits=[last, m_bqc] + qTs_res[ksl].free)
                        bank[bnk].free = [mke]
                        mks = P.dma("gpsimd", s_qT[s, e_, :, ktok],
                                    qT_stg[:, ksl, :], waits=[mke])
                        qTs_res[ksl].free = [mks]
                        m_qTstore[(s, e_, blk)] = mks
                    else:
                        mke = P.op("scalar",
                                   lambda e, bnk=bnk, ksl=ksl, e_=e_:
                                   e.activation(
                                       out=kT_stg[:, ksl, :], in_=pb[bnk][:],
                                       func=AF.Identity,
                                       bias=bk_col[:, e_:e_ + 1]),
                                   waits=[last, m_bkc] + kTs_res[ksl].free)
                        bank[bnk].free = [mke]
                        mkm = P.op("vector", lambda e, ksl=ksl: e.tensor_reduce(
                            out=stat[:, 14:15], in_=kT_stg[:, ksl, :],
                            axis=AX.X, op=OP.max), waits=[mke])
                        P.op("vector", lambda e, e_=e_: e.tensor_max(
                            out=Mk[:, e_:e_ + 1], in0=Mk[:, e_:e_ + 1],
                            in1=stat[:, 14:15]), waits=[mkm, m_mkinit])
                        mks = P.dma("gpsimd", s_kT[s, e_, :, ktok],
                                    kT_stg[:, ksl, :], waits=[mke])
                        kTs_res[ksl].free = [mks, mkm]
                        m_kstore[(s, e_, blk)] = mks
            for t4 in range(4):
                xnT_res[t4].free = [klast]

        nch = min(nchunks, CPS)
        st1 = {0: S1(0)} if nch > 0 else {}
        st2 = {}
        st3 = {}
        for c in range(nch):
            st2[c] = S2(c, st1[c])
            if c - 1 >= 0:
                S4(c - 1, st3[c - 1])
            if c + 1 < nch:
                st1[c + 1] = S1(c + 1)
            st3[c] = S3(c, st2[c])
            if c % 4 == 3 and (c + 1) * 128 <= T:
                kblock(c // 4)
            if s == 1:
                pump_cast()
        for c in range(max(nch - 1, 0), nch):
            S4(c, st3[c])
        if nch >= CPS:
            m_negMk[s] = P.op("vector", lambda e: e.tensor_scalar_mul(
                out=nMk[:], in0=Mk[:], scalar1=-1.0))
        m_ph1end[s] = P.cur("te")

    ph2_state = {}

    def phase2_prefetch(s):
        ek_res = [Res() for _ in range(2)]
        kTe_res = [Res(), Res()]
        ekq_res = [Res(), Res()]
        vp_res = [Res(), Res()]
        loads = {}

        def load_pair(p_):
            psl = p_ % 2
            mload = P.dma("sync", kTe2[:, psl, :], s_kT[s, p_, :, :],
                          waits=[m_kstore[(s, p_, blk)] for blk in range(4)]
                          + kTe_res[psl].free)
            mvp = P.dma("sync", v_p2[:, psl, :, :],
                        s_v[s].rearrange("(c p) j -> p c j", p=128)[
                            :, :, p_ * 128:(p_ + 1) * 128],
                        waits=[m_vstore[(s, c)] for c in range(CPS)]
                        + [m_rstore[(s, c)] for c in (CPS - 2, CPS - 1)]
                        + vp_res[psl].free)
            loads[p_] = (mload, mvp)

        ph2_state[s] = (ek_res, kTe_res, ekq_res, vp_res, loads, load_pair)
        load_pair(0)
        load_pair(1)

    def phase2(s):
        (ek_res, kTe_res, ekq_res, vp_res, loads, load_pair) = ph2_state[s]
        mneg = m_negMk[s]
        for p_ in range(8):
            psl = p_ % 2
            mload, mvp = loads[p_]
            abnk = 2 + (p_ % 2)
            last_mm = None
            exp_marks = []
            msk_last = None
            trs_q = {}

            def emit_exp_tr(qt):
                nonlocal msk_last
                esl2 = qt % 2
                mexp = P.op("scalar", lambda e, p_=p_, psl=psl, qt=qt,
                            esl2=esl2: e.activation(
                                out=ekTq[:, esl2, :],
                                in_=kTe2[:, psl, qt * 512:(qt + 1) * 512],
                                func=AF.Exp, bias=nMk[:, p_:p_ + 1]),
                            waits=[mload, mneg] + ekq_res[esl2].free)
                exp_marks.append(mexp)
                if qt == 0:
                    msk_last = P.op("vector",
                                    lambda e, esl2=esl2: e.tensor_reduce(
                                        out=stat[:, 15:16],
                                        in_=ekTq[:, esl2, :], axis=AX.X,
                                        op=OP.add), waits=[mexp])
                else:
                    mr1 = P.op("vector", lambda e, esl2=esl2: e.tensor_reduce(
                        out=stat[:, 13:14], in_=ekTq[:, esl2, :], axis=AX.X,
                        op=OP.add), waits=[mexp])
                    msk_last = P.op("vector", lambda e: e.tensor_add(
                        out=stat[:, 15:16], in0=stat[:, 15:16],
                        in1=stat[:, 13:14]), waits=[mr1])
                btr = qt % 2
                last_tr = None
                for t4 in range(4):
                    last_tr = P.op("tensor", lambda e, t4=t4, btr=btr,
                                   esl2=esl2: e.transpose(
                                       pb[btr][:, t4 * 128:(t4 + 1) * 128],
                                       ekTq[:, esl2, t4 * 128:(t4 + 1) * 128],
                                       ident[:]),
                                   waits=([mexp, m_ident] + bank[btr].free
                                          + ek_res[qt % 2].free)
                                   if t4 == 0 else [mexp],
                                   inc=(t4 == 3))
                trs_q[qt] = last_tr
                ekq_res[esl2].free = [last_tr, msk_last]

            emit_exp_tr(0)
            for qt in range(4):
                if qt + 1 < 4:
                    emit_exp_tr(qt + 1)
                btr = qt % 2
                mev = P.op("vector", lambda e, btr=btr, qt=qt: e.tensor_copy(
                    out=ek_r.ap()[:, qt % 2, :, :].rearrange(
                        "p a b -> p (a b)"),
                    in_=pb[btr][:]), waits=[trs_q[qt]] + ek_res[qt % 2].free)
                bank[btr].free = [mev]
                for t4 in range(4):
                    t = qt * 4 + t4
                    mmm = P.op("tensor", lambda e, t=t, t4=t4, qt=qt, psl=psl,
                               abnk=abnk: e.matmul(
                                   pb[abnk][:, 0:128], ek_r[:, qt % 2, t4, :],
                                   v_p2[:, psl, t, :],
                                   start=(t == 0), stop=(t == 15)),
                               waits=[mev, mvp]
                               + (bank[abnk].free if t == 0 else []),
                               inc=(t4 == 3))
                    last_mm = mmm
                ek_res[qt % 2].free = [last_mm]
            kTe_res[psl].free = [exp_marks[-1]]
            vp_res[psl].free = [last_mm]
            if p_ + 2 < 8:
                load_pair(p_ + 2)
            pump_cast()
            mrk = P.op("vector", lambda e, p_=p_: e.reciprocal(
                out=rk[:, p_:p_ + 1], in_=stat[:, 15:16]), waits=[msk_last])
            ma1 = P.op("vector", lambda e, p_=p_, abnk=abnk: e.tensor_scalar_mul(
                out=attn_all[0:64, p_, 0:64], in0=pb[abnk][0:64, 0:64],
                scalar1=rk[0:64, p_:p_ + 1]), waits=[last_mm, mrk, m_az])
            ma2 = P.op("vector", lambda e, p_=p_, abnk=abnk: e.tensor_scalar_mul(
                out=attn_all[64:128, p_, 64:128], in0=pb[abnk][64:128, 64:128],
                scalar1=rk[64:128, p_:p_ + 1]), waits=[last_mm, mrk])
            bank[abnk].free = [ma1, ma2]
            m_attn[(s, p_)] = [ma1, ma2]

    def phase3(s):
        qc_res = [Res(), Res()]
        rc_res = [Res(), Res()]
        h1_res = [Res(), Res()]
        hnst_free = []
        hn_prev_act = []
        attn_deps = [m for p_ in range(8) for m in m_attn[(s, p_)]]
        pending_hn = [None]

        def flush_hn():
            if pending_hn[0] is not None:
                pending_hn[0]()
                pending_hn[0] = None

        def make_hn_flush(c, mhn):
            hsl = c % 2
            csl = c % 2

            def emit():
                nonlocal hnst_free, hn_prev_act
                ths = []
                for half in range(2):
                    bnk = half
                    mlast = None
                    for j in range(4):
                        d = half * 4 + j
                        mlast = P.op("tensor",
                                     lambda e, bnk=bnk, j=j, d=d, hsl=hsl:
                                     e.transpose(
                                         pb[bnk][:, j * 128:(j + 1) * 128],
                                         h1n_t[:, hsl, d * 128:(d + 1) * 128],
                                         ident[:]),
                                     waits=[mhn] + bank[bnk].free
                                     + (hnst_free if c % 2 == 0 else []),
                                     inc=(j == 3))
                    ths.append(mlast)
                mc0 = P.op("scalar", lambda e, hsl=hsl: e.activation(
                    out=h1n_t[:, hsl, 0:512], in_=pb[0][:], func=AF.Copy),
                    waits=[ths[0]])
                mc1 = P.op("vector", lambda e, hsl=hsl: e.tensor_copy(
                    out=h1n_t[:, hsl, 512:1024], in_=pb[1][:]), waits=[ths[1]])
                mhi3 = P.op("scalar", lambda e, csl=csl, hsl=hsl: e.activation(
                    out=hnT_hi_st.ap()[:, :, csl * 128:(csl + 1) * 128],
                    in_=h1n_t.ap()[:, hsl, :].rearrange("p (a b) -> p a b",
                                                        b=128),
                    func=AF.Copy), waits=[mc0, mc1])
                mlo3 = P.op("vector", lambda e, csl=csl, hsl=hsl: e.tensor_sub(
                    out=hnT_lo_st.ap()[:, :, csl * 128:(csl + 1) * 128],
                    in0=h1n_t.ap()[:, hsl, :].rearrange("p (a b) -> p a b",
                                                        b=128),
                    in1=hnT_hi_st.ap()[:, :, csl * 128:(csl + 1) * 128]),
                    waits=[mhi3])
                bank[0].free = [mc0]
                bank[1].free = [mc1]
                hn_prev_act = [mhi3, mlo3]
                if c % 2 == 1:
                    tb = s * T + (c - 1) * 128
                    mh_s = P.dma("gpsimd",
                                 s_hnT_hi[:, :, tb:tb + 256].transpose(
                                     [1, 0, 2]),
                                 hnT_hi_st[:], waits=[mhi3, mlo3])
                    ml_s = P.dma("gpsimd",
                                 s_hnT_lo[:, :, tb:tb + 256].transpose(
                                     [1, 0, 2]),
                                 hnT_lo_st[:], waits=[mhi3, mlo3])
                    hnst_free = [mh_s, ml_s]
                    m_hnT_store.append([mh_s, ml_s])
            return emit

        for c in range(CPS):
            sl = c % 2
            tok0 = s * T + c * 128
            mq = P.dma("sync", qT_c[:, sl, :, :],
                       s_qT[s, :, :, c * 128:(c + 1) * 128].transpose(
                           [1, 0, 2]),
                       waits=[m_qTstore[(s, e_, c // 4)] for e_ in range(8)]
                       + qc_res[sl].free)
            mr = P.dma("sync", r_c[:, sl, :], s_r[s, c * 128:(c + 1) * 128, :],
                       waits=[m_rstore[(s, c)]] + rc_res[sl].free)
            b0 = 4 if sl == 0 else 6
            sqm = None
            for p_ in range(8):
                sqm = P.op("tensor", lambda e, p_=p_, sl=sl: e.matmul(
                    pb[2][:, p_ * 2:(p_ + 1) * 2], qT_c[:, sl, p_, :],
                    ones2[:, :], start=True, stop=True),
                    waits=([mq] + bank[2].free) if p_ == 0 else (),
                    inc=(p_ == 7))
            msqe = P.op("vector", lambda e: e.tensor_copy(
                out=stat[:, 16:32], in_=pb[2][:, 0:16]), waits=[sqm])
            bank[2].free = [msqe]
            mrq = P.op("vector", lambda e, c=c: e.reciprocal(
                out=rq_all[:, c, :], in_=stat[:, 16:32]), waits=[msqe])
            yms = []
            for p_ in range(8):
                bnk = b0 + p_ // 4
                mmy = P.op("tensor", lambda e, p_=p_, bnk=bnk, sl=sl: e.matmul(
                    pb[bnk][:, (p_ % 4) * 128:(p_ % 4 + 1) * 128],
                    qT_c[:, sl, p_, :], attn_all[:, p_, :],
                    start=True, stop=True),
                    waits=[mq] + attn_deps
                    + (bank[bnk].free if p_ % 4 == 0 else []),
                    inc=(p_ % 4 == 3))
                if p_ % 4 == 3:
                    yms.append(mmy)
            flush_hn()
            hevs = []
            bfree = []
            for half in range(2):
                bnk = b0 + half
                hs = slice(half * 512, (half + 1) * 512)
                rq_bc = bass.AP(rq_all.ap().tensor, c * 16 + half * 8,
                                [[CPS * 16, 128], [1, 8], [0, 64]])
                mmul = P.op("vector",
                            lambda e, bnk=bnk, hs=hs, sl=sl, rq_bc=rq_bc:
                            e.tensor_tensor(
                                out=h1_st.ap()[:, sl, hs].rearrange(
                                    "p (h x) -> p h x", x=64),
                                in0=pb[bnk].ap()[:, :].rearrange(
                                    "p (h x) -> p h x", x=64),
                                in1=rq_bc, op=OP.mult),
                            waits=[yms[half], mrq] + h1_res[sl].free)
                bfree.append(mmul)
                madd = P.op("vector",
                            lambda e, hs=hs, sl=sl: e.tensor_add(
                                out=h1_st[:, sl, hs],
                                in0=h1_st[:, sl, hs],
                                in1=r_c[:, sl, hs]),
                            waits=[mmul, mr])
                hevs.append(madd)
            bank[b0].free = [bfree[0]]
            bank[b0 + 1].free = [bfree[1]]
            rc_res[sl].free = [hevs[-1]]
            qc_res[sl].free = yms[:] + [sqm]
            mh1 = P.dma("gpsimd", s_h1[tok0:tok0 + 128, :], h1_st[:, sl, :],
                        waits=hevs)
            m_h1store[(s, c)] = mh1
            h1_res[sl].free = [mh1]

            mrc3 = ln_stats(h1_st.ap()[:, sl, :], hevs, 0, 4)
            mnb3 = P.op("vector", lambda e: e.tensor_scalar(
                out=stat[:, 10:11], in0=stat[:, 0:1], scalar1=stat[:, 4:5],
                scalar2=-1.0, op0=OP.mult, op1=OP.mult), waits=[mrc3])
            mhn = P.op("scalar", lambda e, sl=sl, c=c: e.activation(
                out=h1n_t[:, c % 2, :], in_=h1_st[:, sl, :], func=AF.Identity,
                bias=stat[:, 10:11], scale=stat[:, 4:5]),
                waits=[mnb3] + hn_prev_act)
            pending_hn[0] = make_hn_flush(c, mhn)
        flush_hn()

    stages = []
    for s in range(NSEQ):
        stages += [(f"p1s{s}", phase1), (f"p2s{s}", phase2), (f"p3s{s}", phase3)]
    early_ffn = {"m": None}

    def emit_early_ffn_loads():
        deps = [m_ph1end[1]]
        mw = {}
        for d in range(8):
            for half in range(2):
                rows = slice(half * 1536, (half + 1) * 1536)
                mw[("g", d, half)] = P.dma(
                    "scalar", wgT[:, d, rows], s_wg[rows, d * 128:(d + 1) * 128],
                    waits=deps + m_cast["g"], transpose=True)
                mw[("u", d, half)] = P.dma(
                    "scalar", wuT[:, d, rows], s_wu[rows, d * 128:(d + 1) * 128],
                    waits=m_cast["u"] + deps, transpose=True)
        early_ffn["m"] = mw

    ffn_pre = {}

    def prefetch_ffn_blocks():
        (ek_res2, kTe_res2, ekq_res2, vp_res2, _, _) = ph2_state[1]
        ph2done = (vp_res2[0].free + vp_res2[1].free
                   + kTe_res2[0].free + kTe_res2[1].free)
        for b in range(2):
            t0b = b * 256
            mhi = P.dma("sync", hnT_hi_b[:, b % 2, :, :],
                        s_hnT_hi[:, :, t0b:t0b + 256].transpose([1, 0, 2]),
                        waits=m_hnT_store[b] + ph2done)
            mlo = P.dma("sync", hnT_lo_b[:, b % 2, :, :],
                        s_hnT_lo[:, :, t0b:t0b + 256].transpose([1, 0, 2]),
                        waits=m_hnT_store[b] + ph2done)
            ffn_pre[b] = [mhi, mlo]

    if upto != "all":
        emit_casts()
    done = upto == "prologue"
    for nm, fn in stages:
        if done:
            break
        fn(int(nm[-1]))
        if nm.startswith("p1") and upto == "all":
            pre = P.fence_markers()
            phase2_prefetch(int(nm[-1]))
            P.fence(pre)
        elif nm == "p2s1" and upto == "all":
            pre = P.fence_markers()
            emit_casts()
            emit_early_ffn_loads()
            prefetch_ffn_blocks()
            P.fence(pre)
        elif nm.startswith("p2"):
            pass
        else:
            P.fence()
        if nm == upto:
            done = True

    if done:
        P.op("gpsimd", lambda e: e.nop(),
             waits=P.queue_tail("gpsimd") + P.queue_tail("sync"), inc=False)
        with nc.Block() as block:
            P.emit(block)
        return nc
    mw = early_ffn["m"]
    m_wg = {(d, h): mw[("g", d, h)] for d in range(8) for h in range(2)}
    m_wu = {(d, h): mw[("u", d, h)] for d in range(8) for h in range(2)}
    m_wd = {}
    for hc in range(24):
        m_wd[hc] = P.dma(
            "sync", wdT_t[:, hc, :], s_wd[0:D, hc * 128:(hc + 1) * 128],
            waits=m_cast["d"], transpose=True)

    hb_res = [Res(), Res()]
    h1b_res = [Res() for _ in range(4)]
    g1_res = [Res(), Res()]
    hm_res = [Res(), Res()]
    out_res = [Res(), Res()]

    NBLK = TOK // 256
    hm_res4 = [Res() for _ in range(4)]
    for b in range(NBLK):
        bsl = b % 2
        t0 = b * 256
        if b in ffn_pre:
            mhi, mlo = ffn_pre[b]
        else:
            mhi = P.dma("sync", hnT_hi_b[:, bsl, :, :],
                        s_hnT_hi[:, :, t0:t0 + 256].transpose([1, 0, 2]),
                        waits=m_hnT_store[b] + hb_res[bsl].free)
            mlo = P.dma("sync", hnT_lo_b[:, bsl, :, :],
                        s_hnT_lo[:, :, t0:t0 + 256].transpose([1, 0, 2]),
                        waits=m_hnT_store[b] + hb_res[bsl].free)
        mh1l = []
        for tc in range(2):
            hsl = (2 * b + tc) % 4
            sseq = t0 // T
            cch = (t0 % T) // 128 + tc
            ml = P.dma("sync", h1_b[:, hsl, :],
                       s_h1[t0 + tc * 128:t0 + (tc + 1) * 128, :],
                       waits=[m_h1store[(sseq, cch)]] + h1b_res[hsl].free)
            mh1l.append(ml)

        down_banks = [4, 5] if bsl == 0 else [6, 7]
        dn_last = {}
        pending_dn = [None]

        def flush_dn():
            if pending_dn[0] is not None:
                pending_dn[0]()
                pending_dn[0] = None

        def make_dn(hc, hsl4, mhh, mhl, down_banks=down_banks, dn_last=dn_last):
            def emit():
                dlast = None
                for tc in range(2):
                    for (hmt, mdep) in ((hmT_hi, mhh), (hmT_lo, mhl)):
                        for half in range(2):
                            bnk = (down_banks if tc == 0
                                   else [down_banks[0] ^ 2,
                                         down_banks[1] ^ 2])[half]
                            st = (hc == 0 and hmt is hmT_hi)
                            sp = (hc == 23 and hmt is hmT_lo)
                            dlast = P.op(
                                "tensor",
                                lambda e, hmt=hmt, hsl4=hsl4, tc=tc, hc=hc,
                                half=half, bnk=bnk, st=st, sp=sp: e.matmul(
                                    pb[bnk][:],
                                    hmt[:, hsl4, tc * 128:(tc + 1) * 128],
                                    wdT(hc)[:, half * 512:(half + 1) * 512],
                                    start=st, stop=sp),
                                waits=[mdep, m_wd[hc]]
                                + (bank[bnk].free if hc == 0 else []),
                                inc=(hc == 23))
                            if hc == 23:
                                dn_last[(tc, half)] = dlast
                hm_res4[hsl4].free = [dlast]
            return emit

        for hc in range(24):
            gsl = hc % 2
            hsl4 = hc % 4
            gb = gsl
            ub = 2 + gsl
            wdeps = ([m_wg[(d, h)] for d in range(8) for h in range(2)]
                     + [m_wu[(d, h)] for d in range(8) for h in range(2)]) \
                if (b == 0 and hc == 0) else []
            n = 0
            mgu = None
            for d in range(8):
                for wtt, bnk in ((wgT, gb), (wuT, ub)):
                    for xt in (hnT_hi_b, hnT_lo_b):
                        n += 1
                        st = (d == 0 and xt is hnT_hi_b)
                        sp = (d == 7 and xt is hnT_lo_b)
                        mgu = P.op("tensor",
                                   lambda e, d=d, wtt=wtt, bnk=bnk, xt=xt,
                                   hc=hc, bsl=bsl, st=st, sp=sp: e.matmul(
                                       pb[bnk][:, 0:256],
                                       wtt[:, d, hc * 128:(hc + 1) * 128],
                                       xt.ap()[:, bsl, d, :],
                                       start=st, stop=sp),
                                   waits=(wdeps + [mhi, mlo] + bank[gb].free
                                          + bank[ub].free) if n == 1 else (),
                                   inc=(n == 32))
            flush_dn()
            mg1 = P.op("scalar", lambda e, gb=gb, gsl=gsl: e.activation(
                out=g1_t[:, gsl, :], in_=pb[gb][:, 0:256], func=AF.Silu,
                scale=QS), waits=[mgu] + g1_res[gsl].free)
            mhm = P.op("vector",
                       lambda e, ub=ub, gsl=gsl: e.scalar_tensor_tensor(
                           out=g1_t[:, gsl, :], in0=pb[ub][:, 0:256], scalar=QS,
                           in1=g1_t[:, gsl, :], op0=OP.mult, op1=OP.mult),
                       waits=[mg1])
            bank[gb].free = [mhm]
            bank[ub].free = [mhm]
            mhh = P.op("vector", lambda e, gsl=gsl, hsl4=hsl4: e.tensor_copy(
                out=hmT_hi[:, hsl4, :], in_=g1_t[:, gsl, :]),
                waits=[mhm] + hm_res4[hsl4].free)
            mhl = P.op("vector", lambda e, gsl=gsl, hsl4=hsl4: e.tensor_sub(
                out=hmT_lo[:, hsl4, :], in0=g1_t[:, gsl, :],
                in1=hmT_hi[:, hsl4, :]), waits=[mhh])
            g1_res[gsl].free = [mhl]
            pending_dn[0] = make_dn(hc, hsl4, mhh, mhl)
        flush_dn()

        for tc in range(2):
            osl = tc
            hsl = (2 * b + tc) % 4
            oevs = []
            for half in range(2):
                bnk = (down_banks if tc == 0
                       else [down_banks[0] ^ 2, down_banks[1] ^ 2])[half]
                mo = P.op("vector",
                          lambda e, bnk=bnk, osl=osl, hsl=hsl, half=half:
                          e.scalar_tensor_tensor(
                              out=out_stg[:, osl, half * 512:(half + 1) * 512],
                              in0=pb[bnk][:], scalar=QS,
                              in1=h1_b.ap()[:, hsl,
                                            half * 512:(half + 1) * 512],
                              op0=OP.mult, op1=OP.add),
                          waits=[dn_last[(tc, half)], mh1l[tc]]
                          + out_res[osl].free)
                bank[bnk].free = [mo]
                oevs.append(mo)
            most = P.dma("gpsimd",
                         out_hbm[t0 + tc * 128:t0 + (tc + 1) * 128, :],
                         out_stg[:, osl, :], waits=oevs)
            out_res[osl].free = [most]
            h1b_res[hsl].free = [most]
        hb_res[bsl].free = [dn_last[(1, 1)]]

    P.op("gpsimd", lambda e: e.nop(),
         waits=P.queue_tail("gpsimd") + P.queue_tail("sync")
         + P.queue_tail("scalar"), inc=False)

    with nc.Block() as block:
        P.emit(block)
    return nc


_NC_CACHE = {}


def kernel(**inputs):
    x = np.ascontiguousarray(np.asarray(inputs["L0_attn_x"], dtype=np.float32))
    B = x.shape[0]
    per = B // NCORES
    key = "main"
    if key not in _NC_CACHE:
        _NC_CACHE[key] = build()
    nc = _NC_CACHE[key]
    common = {
        "ln_g": np.asarray(inputs["L0_attn_ln_g"], dtype=np.float32),
        "ln_b": np.asarray(inputs["L0_attn_ln_b"], dtype=np.float32),
        "Wq": np.asarray(inputs["L0_attn_Wq"], dtype=np.float32),
        "Wk": np.asarray(inputs["L0_attn_Wk"], dtype=np.float32),
        "Wv": np.asarray(inputs["L0_attn_Wv"], dtype=np.float32),
        "bq": np.asarray(inputs["L0_attn_bq"], dtype=np.float32),
        "bk": np.asarray(inputs["L0_attn_bk"], dtype=np.float32),
        "bv": np.asarray(inputs["L0_attn_bv"], dtype=np.float32),
        "wg": np.asarray(inputs["L0_ffn_w_gate"], dtype=np.int32),
        "wu": np.asarray(inputs["L0_ffn_w_up"], dtype=np.int32),
        "wd": np.asarray(inputs["L0_ffn_w_down"], dtype=np.int32),
        "ident": np.eye(128, dtype=np.float32),
    }
    in_maps = []
    for c in range(NCORES):
        shard = np.ascontiguousarray(
            x[c * per:(c + 1) * per].reshape(per * x.shape[1], x.shape[2]))
        m = dict(common)
        m["x"] = shard
        in_maps.append(m)
    res = run_bass_kernel_spmd(nc, in_maps, core_ids=list(range(NCORES)))
    out = np.concatenate([np.asarray(r["out"]) for r in res.results], axis=0)
    return out.reshape(B, x.shape[1], x.shape[2])
